# version 23
# baseline (speedup 1.0000x reference)
import numpy as np
import ml_dtypes
import concourse.bass as bass
import concourse.mybir as mybir
from concourse.bass_utils import run_bass_kernel_spmd
from contextlib import ExitStack

F32 = mybir.dt.float32
F32R = mybir.dt.float32r
BF16 = mybir.dt.bfloat16
FP16 = mybir.dt.float16
AF = mybir.ActivationFunctionType
ALU = mybir.AluOpType
AX = mybir.AxisListType
BFNP = ml_dtypes.bfloat16

NC = 8
D = 1024
T = 4096
H = 16
DH = 64
R = 16
NF = 8
CH = 256
NCH = T // CH
KT = 128
NKT = T // KT
NB = 4
EPS = 1e-5

ENGS = ("sync", "gp", "ve", "sc", "tn")
SEM_NAMES = ("IN", "SP", "TN", "VE", "SC")


class Sched:
    def __init__(self):
        self.ops = {e: [] for e in ENGS}
        self.cnt = {s: 0 for s in SEM_NAMES}

    def wait(self, eng, sem, val=None):
        v = self.cnt[sem] if val is None else val
        if v > 0:
            self.ops[eng].append(("w", sem, v))

    def op(self, eng, fn, sem=None, amt=1):
        if sem is not None:
            self.cnt[sem] += amt
            self.ops[eng].append(("i", fn, sem, amt))
            return self.cnt[sem]
        self.ops[eng].append(("i", fn, None, 0))
        return None

    def dma(self, eng, out, in_, sem):
        return self.op(eng, (lambda e, o=out, i=in_: e.dma_start(out=o, in_=i)), sem, 16)

    def barrier(self):
        snap = dict(self.cnt)
        for e in ENGS:
            for s in SEM_NAMES:
                self.wait(e, s, snap[s])


def build_program():
    nc = bass.Bass("TRN2", target_bir_lowering=False, debug=False,
                   dynamic_dma_scratch_size=16384)

    x_fm = nc.dram_tensor("x_fm", [NF, 128, T], F32, kind="ExternalInput").ap()
    wq_t = nc.dram_tensor("wq_t", [64, H, D], F32, kind="ExternalInput").ap()
    wk_t = nc.dram_tensor("wk_t", [64, H, D], F32, kind="ExternalInput").ap()
    w_v = nc.dram_tensor("w_v", [D, D], F32, kind="ExternalInput").ap()
    qlsr_t = nc.dram_tensor("qlsr_t", [64, H, R], F32, kind="ExternalInput").ap()
    klsr_t = nc.dram_tensor("klsr_t", [64, H, R], F32, kind="ExternalInput").ap()
    core_r = nc.dram_tensor("core_r", [1, H * R], F32, kind="ExternalInput").ap()
    w_o = nc.dram_tensor("w_o", [D, D], F32, kind="ExternalInput").ap()
    w_fr = nc.dram_tensor("w_fr", [D, D], F32, kind="ExternalInput").ap()
    w_fi = nc.dram_tensor("w_fi", [D, D], F32, kind="ExternalInput").ap()
    biasv = nc.dram_tensor("biasv", [128, NF], F32, kind="ExternalInput").ap()
    cmat = nc.dram_tensor("cmat", [D, D], F32, kind="ExternalInput").ap()
    smat = nc.dram_tensor("smat", [D, D], F32, kind="ExternalInput").ap()
    cneg = nc.dram_tensor("cneg", [D, D], F32, kind="ExternalInput").ap()
    masks = nc.dram_tensor("masks", [128, CH], BF16, kind="ExternalInput").ap()
    selmat = nc.dram_tensor("selmat", [128, NB], FP16, kind="ExternalInput").ap()
    e_mat = nc.dram_tensor("e_mat", [NB, 128], FP16, kind="ExternalInput").ap()
    onesrow = nc.dram_tensor("onesrow", [1, 128], F32, kind="ExternalInput").ap()
    onesb = nc.dram_tensor("onesb", [128, 1], FP16, kind="ExternalInput").ap()

    out_fm = nc.dram_tensor("out_fm", [NF, 128, T], F32, kind="ExternalOutput").ap()

    at_sp = nc.dram_tensor("at_sp", [NF, 128, D], F32, kind="Internal").ap()
    nbt_sp = nc.dram_tensor("nbt_sp", [NF, 128, D], F32, kind="Internal").ap()
    ml_sp2 = nc.dram_tensor("ml_sp2", [NF, 128, NF, 128], FP16, kind="Internal").ap()

    top = ExitStack()
    sems = {s: top.enter_context(nc.semaphore(s)) for s in SEM_NAMES}
    S = Sched()
    W, O, DMA = S.wait, S.op, S.dma
    sb = lambda name, shape, dt: top.enter_context(nc.sbuf_tensor(name, shape, dt))

    def DMA8(eng, dst_fn, src_fn, sem):
        for i in range(NF):
            v = DMA(eng, dst_fn(i), src_fn(i), sem)
        return v

    p_s = top.enter_context(nc.psum_tensor("p_s", [128, NB, 512], F32))
    p_att = top.enter_context(nc.psum_tensor("p_att", [65, 2, 2, CH], F32))
    p_proj = top.enter_context(nc.psum_tensor("p_proj", [128, 512], F32))
    p_misc = top.enter_context(nc.psum_tensor("p_misc", [128, 512], F32))

    s_mhb = sb("s_mhb", [128, NF, D], BF16)
    s_mlp = sb("s_mlp", [128, 2, NF, 128], FP16)
    s_A = sb("s_A", [128, NF, NB, 128], FP16)
    s_B = sb("s_B", [128, NF, NB, 128], FP16)
    s_onesr = sb("s_onesr", [1, 128], F32)
    s_onesb = sb("s_onesb", [128, 1], FP16)

    DMA("sync", s_onesr[:], onesrow[:, :], "IN")
    DMA("sync", s_onesb[:], onesb[:, :], "IN")

    with ExitStack() as ph0:
        s_wqT = ph0.enter_context(nc.sbuf_tensor("s_wqT", [64, H, D], F32))
        s_lsr = ph0.enter_context(nc.sbuf_tensor("s_lsr", [64, 2, H, R], F32))
        s_qsc = ph0.enter_context(nc.sbuf_tensor("s_qsc", [64, H, R], F32))
        s_cb = ph0.enter_context(nc.sbuf_tensor("s_cb", [1, H * R], F32))

        DMA("sync", s_lsr[:, 0, :, :], qlsr_t[:, :, :], "IN")
        DMA("sync", s_lsr[:, 1, :, :], klsr_t[:, :, :], "IN")
        core_ld = DMA("sync", s_cb[:], core_r[:, :], "IN")
        wq_ld = DMA("gp", s_wqT[:], wq_t[:, :, :], "SP")

        W("tn", "IN", core_ld)
        cb = O("tn", (lambda e: e.matmul(p_misc[0:64, 0:H * R], s_onesr[:, 0:64],
                                         s_cb[:, :], start=True, stop=True,
                                         skip_group_check=True)), "TN")
        W("ve", "TN", cb)
        O("ve", (lambda e: e.tensor_scalar(
            s_qsc[:].rearrange("p h r -> p (h r)"), s_lsr[:, 0, :, :].rearrange("p h r -> p (h r)"),
            0.25, None, ALU.mult)), "VE")
        qsc_done = O("ve", (lambda e: e.tensor_mul(
            s_qsc[:].rearrange("p h r -> p (h r)"), s_qsc[:].rearrange("p h r -> p (h r)"),
            p_misc[0:64, 0:H * R])), "VE")

        O("ve", (lambda e: e.memset(s_A[:], 0.0)), "VE")
        ab_clear = O("ve", (lambda e: e.memset(s_B[:], 0.0)), "VE")
        for side, dst in ((0, s_A), (1, s_B)):
            W("tn", "SP", wq_ld)
            if side == 0:
                W("tn", "VE", qsc_done)
            guard = ab_clear
            for h in range(H):
                b, g = h // 4, h % 4
                for fi in range(NF):
                    W("tn", "VE", guard)
                    mm = O("tn", (lambda e, h=h, fi=fi, side=side: e.matmul(
                        p_proj[:, 0:R],
                        s_wqT[:, h, fi * 128:(fi + 1) * 128],
                        (s_qsc[:, h, :] if side == 0 else s_lsr[:, 1, h, :]),
                        start=True, stop=True, skip_group_check=True)), "TN")
                    W("ve", "TN", mm)
                    guard = O("ve", (lambda e, dst=dst, fi=fi, b=b, g=g: e.tensor_copy(
                        dst[:, fi, b, 32 * g:32 * g + 16],
                        p_proj[:, 0:R])), "VE")
            if side == 0:
                W("gp", "VE", guard)
                wq_ld = DMA("gp", s_wqT[:], wk_t[:, :, :], "SP")
        S.barrier()

    with ExitStack() as phb:
        s_C = phb.enter_context(nc.sbuf_tensor("s_C", [128, NF, D], F32R))
        s_Sm = phb.enter_context(nc.sbuf_tensor("s_Sm", [128, NF, D], F32R))
        cm_ld = DMA8("gp", lambda i: s_C[:, i, :], lambda i: cmat[i * 128:(i + 1) * 128, :], "SP")
        sm_ld = DMA8("gp", lambda i: s_Sm[:, i, :], lambda i: smat[i * 128:(i + 1) * 128, :], "SP")

        bank = [0]
        evsnap = [0, 0, 0]

        def stage_mm(terms, dst_sp, s_tmp):
            tsp_free = [0, 0]
            for oc in range(NF):
                for wi in range(2):
                    b = bank[0] % 3
                    bank[0] += 1
                    W("tn", "VE", evsnap[b])
                    nterm = len(terms)
                    for ti, (lt, rt) in enumerate(terms):
                        for dc in range(NF):
                            tv = O("tn", (lambda e, lt=lt, rt=rt, dc=dc, oc=oc, wi=wi, b=b,
                                          st=(ti == 0 and dc == 0),
                                          sp=(ti == nterm - 1 and dc == NF - 1): e.matmul(
                                p_s[:, b, :],
                                lt[:, dc, oc * 128:(oc + 1) * 128],
                                rt[:, dc, wi * 512:(wi + 1) * 512],
                                start=st, stop=sp)),
                                "TN" if (ti == nterm - 1 and dc == NF - 1) else None)
                            if tv is not None:
                                t_done = tv
                    W("ve", "TN", t_done)
                    tb = wi
                    W("ve", "SP", tsp_free[tb])
                    ev = O("ve", (lambda e, b=b, tb=tb: e.tensor_copy(
                        s_tmp[tb][:, :], p_s[:, b, :])), "VE")
                    evsnap[b] = ev
                    W("gp", "VE", ev)
                    tsp_free[tb] = DMA("gp", dst_sp[oc, :, wi * 512:(wi + 1) * 512],
                                       s_tmp[tb][:], "SP")

        with ExitStack() as ph1:
            s_Wr = ph1.enter_context(nc.sbuf_tensor("s_Wr", [128, NF, D], F32R))
            s_Wi = ph1.enter_context(nc.sbuf_tensor("s_Wi", [128, NF, D], F32R))
            s_tmp = [ph1.enter_context(nc.sbuf_tensor(f"s_tmp{i}", [128, 512], F32)) for i in range(2)]
            DMA8("gp", lambda i: s_Wr[:, i, :], lambda i: w_fr[i * 128:(i + 1) * 128, :], "SP")
            DMA8("gp", lambda i: s_Wi[:, i, :], lambda i: w_fi[i * 128:(i + 1) * 128, :], "SP")
            W("tn", "SP")
            stage_mm([(s_Wr, s_C), (s_Wi, s_Sm)], at_sp, s_tmp)
            W("gp", "TN")
            cneg_snap = DMA8("gp", lambda i: s_C[:, i, :], lambda i: cneg[i * 128:(i + 1) * 128, :], "SP")
            W("tn", "SP", cneg_snap)
            stage_mm([(s_Wr, s_Sm), (s_Wi, s_C)], nbt_sp, s_tmp)
            S.barrier()
        with ExitStack() as ph2:
            s_AT = ph2.enter_context(nc.sbuf_tensor("s_AT", [128, NF, D], F32R))
            s_NBT = ph2.enter_context(nc.sbuf_tensor("s_NBT", [128, NF, D], F32R))
            s_tf = ph2.enter_context(nc.sbuf_tensor("s_tf", [128, 512], F32))
            s_mlo = [ph2.enter_context(nc.sbuf_tensor(f"s_mlo{i}", [128, 512], FP16)) for i in range(2)]
            mlo_free = [0, 0]
            DMA8("gp", lambda i: s_C[:, i, :], lambda i: cmat[i * 128:(i + 1) * 128, :], "SP")
            DMA8("gp", lambda i: s_AT[:, i, :], lambda i: at_sp[i, :, :], "SP")
            at_snap = DMA8("gp", lambda i: s_NBT[:, i, :], lambda i: nbt_sp[i, :, :], "SP")
            W("tn", "SP", at_snap)
            for oc in range(NF):
                for wi in range(2):
                    b = bank[0] % 3
                    bank[0] += 1
                    W("tn", "VE", evsnap[b])
                    for ti, (lt, rt) in enumerate(((s_AT, s_C), (s_NBT, s_Sm))):
                        for dc in range(NF):
                            tv = O("tn", (lambda e, lt=lt, rt=rt, dc=dc, oc=oc, wi=wi, b=b,
                                          st=(ti == 0 and dc == 0),
                                          sp=(ti == 1 and dc == NF - 1): e.matmul(
                                p_s[:, b, :],
                                lt[:, dc, oc * 128:(oc + 1) * 128],
                                rt[:, dc, wi * 512:(wi + 1) * 512],
                                start=st, stop=sp)),
                                "TN" if (ti == 1 and dc == NF - 1) else None)
                            if tv is not None:
                                t_done = tv
                    W("ve", "TN", t_done)
                    O("ve", (lambda e, b=b: e.tensor_scalar(
                        s_tf[:, :], p_s[:, b, :], 1.0 / D, None, ALU.mult)), "VE")
                    O("ve", (lambda e, oc=oc, wi=wi: e.tensor_copy(
                        s_mhb[:, oc, wi * 512:wi * 512 + 512], s_tf[:, :])), "VE")
                    mb_ = (2 * oc + wi) % 2
                    W("ve", "IN", mlo_free[mb_])
                    ev = O("ve", (lambda e, oc=oc, wi=wi, mb_=mb_: e.tensor_sub(
                        s_mlo[mb_][:, :], s_tf[:, :],
                        s_mhb[:, oc, wi * 512:wi * 512 + 512])), "VE")
                    evsnap[b] = ev
                    W("sync", "VE", ev)
                    for fp in range(4):
                        mlo_free[mb_] = DMA("sync", ml_sp2[4 * wi + fp, :, oc, :],
                                            s_mlo[mb_][:, fp * 128:(fp + 1) * 128], "IN")
            S.barrier()

    s_klr = sb("s_klr", [128, NB, T], FP16)
    v_s = sb("v_s", [128, NKT, H, 65], FP16)
    s_wv = sb("s_wv", [128, NF, D], FP16)
    s_wo = sb("s_wo", [128, NF, D], FP16)
    s_x = [sb(f"s_x{i}", [128, NF, CH], F32) for i in range(2)]
    s_xh = sb("s_xh", [128, NF, CH], BF16)
    s_xl = sb("s_xl", [128, NF, CH], BF16)
    s_qlr = sb("s_qlr", [128, NB, CH], FP16)
    s_ring = sb("s_ring", [128, 2, 2, CH], BF16)
    s_mask = sb("s_mask", [128, CH], BF16)
    s_sq = sb("s_sq", [128, NB, CH], FP16)
    s_afm = sb("s_afm", [128, NF, CH], FP16)
    s_scr = sb("s_scr", [1, 2, CH], F32)
    s_t32 = sb("s_t32", [128, CH], F32)
    s_att = sb("s_att", [64, 2, CH], BF16)
    s_sel = sb("s_sel", [128, NB], FP16)
    s_emat = sb("s_emat", [NB, 128], FP16)
    s_bias = sb("s_bias", [128, NF], F32)
    s_kn = sb("s_kn", [NB, NB], F32)
    s_qn = sb("s_qn", [NB, NB], F32)
    s_cc = sb("s_cc", [NB, NB], F32)
    s_one4 = sb("s_one4", [NB, CH], FP16)
    s_negc = sb("s_negc", [NB, CH], FP16)

    DMA("sync", s_mask[:], masks[:, :], "IN")
    DMA("sync", s_sel[:], selmat[:, :], "IN")
    DMA("sync", s_emat[:], e_mat[:, :], "IN")
    DMA("sync", s_bias[:], biasv[:, :], "IN")
    DMA8("gp", lambda i: s_wv[:, i, :], lambda i: w_v[i * 128:(i + 1) * 128, :], "SP")
    DMA8("gp", lambda i: s_wo[:, i, :], lambda i: w_o[i * 128:(i + 1) * 128, :], "SP")
    x_snap = [0] * (NCH + 1)
    x_snap[0] = DMA8("sync", lambda i: s_x[0][:, i, :], lambda i: x_fm[i, :, 0:CH], "IN")
    x_snap[1] = DMA8("sync", lambda i: s_x[1][:, i, :], lambda i: x_fm[i, :, CH:2 * CH], "IN")

    O("ve", (lambda e: e.memset(s_klr[:], 0.0)), "VE")
    O("ve", (lambda e: e.memset(s_qlr[:], 0.0)), "VE")
    O("ve", (lambda e: e.memset(s_kn[:], 0.0)), "VE")
    O("ve", (lambda e: e.memset(s_one4[:], 1.0)), "VE")
    O("ve", (lambda e: e.memset(v_s[:], 1.0)), "VE")
    S.barrier()

    G = {
        "xh_ready": 0,
        "xh_free": 0,
        "scr_tn": 0,
        "sq_free": 0,
        "x_free": [0, 0],
        "misc_ve": 0,
        "proj_ve": 0,
        "qlr_free": 0,
        "qlr_ready": 0,
        "v_ready": 0,
        "ring_tn": [0, 0],
        "ring_sc": [0, 0],
        "exp_snap": [0, 0],
        "att_ve": [0, 0],
        "afm_tn": 0,
        "mlp_tn": [0, 0],
    }

    def emit_ln(c):
        buf = c % 2
        xb = s_x[buf]
        W("ve", "IN", x_snap[c])
        m2_done = 0
        for half in range(2):
            W("ve", "TN", G["sq_free"] if half == 0 else m2_done)
            cp_done = O("ve", (lambda e, half=half: e.tensor_copy(
                s_sq[:].rearrange("p a t -> p (a t)"),
                xb[:, 4 * half:4 * half + 4, :].rearrange("p a t -> p (a t)"))), "VE")
            W("tn", "VE", cp_done)
            if half == 0:
                W("tn", "VE", G["misc_ve"])
            for j in range(4):
                mean_done = O("tn", (lambda e, half=half, j=j: e.matmul(
                    p_misc[0:1, 0:CH], s_onesb[:, :], s_sq[:, j, :],
                    start=(half == 0 and j == 0), stop=False,
                    skip_group_check=True)), "TN" if j == 3 else None)
            W("ve", "TN", mean_done)
            sq_done = O("ve", (lambda e: e.tensor_mul(
                s_sq[:].rearrange("p a t -> p (a t)"),
                s_sq[:].rearrange("p a t -> p (a t)"),
                s_sq[:].rearrange("p a t -> p (a t)"))), "VE")
            W("tn", "VE", sq_done)
            for j in range(4):
                mv = O("tn", (lambda e, half=half, j=j: e.matmul(
                    p_misc[0:1, CH:2 * CH], s_onesb[:, :], s_sq[:, j, :],
                    start=False, stop=(half == 1 and j == 3),
                    skip_group_check=True)), "TN" if j == 3 else None)
                if mv is not None:
                    m2_done = mv
        G["sq_free"] = m2_done
        W("ve", "TN", m2_done)
        W("ve", "TN", G["scr_tn"])
        O("ve", (lambda e: e.tensor_copy(
            s_scr[:, 0:2, :], p_misc[0:1, 0:2 * CH].rearrange("p (a t) -> p a t", a=2))), "VE")
        O("ve", (lambda e: e.tensor_mul(s_t32[0:1, :], s_scr[:, 0, :], s_scr[:, 0, :])), "VE")
        O("ve", (lambda e: e.tensor_sub(s_scr[:, 1, :], s_scr[:, 1, :], s_t32[0:1, :])), "VE")
        var_done = O("ve", (lambda e: e.tensor_scalar_add(s_scr[:, 1, :], s_scr[:, 1, :], EPS)), "VE")
        W("sc", "VE", var_done)
        O("sc", (lambda e: e.activation(s_scr[:, 1, :], s_scr[:, 1, :], AF.Ln)), "SC")
        rstd_done = O("sc", (lambda e: e.activation(s_scr[:, 1, :], s_scr[:, 1, :], AF.Exp, scale=-0.5)), "SC")
        W("tn", "SC", rstd_done)
        W("tn", "VE", G["misc_ve"])
        O("tn", (lambda e: e.matmul(p_misc[:, 0:CH], s_onesr[:, :],
                                    s_scr[:, 0, :], start=True, stop=False,
                                    skip_group_check=True)), "TN")
        bc_done = O("tn", (lambda e: e.matmul(p_misc[:, CH:2 * CH], s_onesr[:, :],
                                              s_scr[:, 1, :], start=False, stop=True,
                                              skip_group_check=True)), "TN")
        G["scr_tn"] = bc_done
        W("ve", "TN", bc_done)
        W("ve", "TN", G["xh_free"])
        for fi in range(NF):
            O("ve", (lambda e, fi=fi: e.tensor_sub(s_t32[:, :], xb[:, fi, :], p_misc[:, 0:CH])), "VE")
            O("ve", (lambda e, fi=fi: e.tensor_mul(s_t32[:, :], s_t32[:, :], p_misc[:, CH:2 * CH])), "VE")
            O("ve", (lambda e, fi=fi: e.tensor_copy(s_xh[:, fi, :], s_t32[:, :])), "VE")
            xn_done = O("ve", (lambda e, fi=fi: e.tensor_sub(s_xl[:, fi, :], s_t32[:, :], s_xh[:, fi, :])), "VE")
        G["misc_ve"] = xn_done
        G["xh_ready"] = xn_done
        for fi in range(NF):
            O("ve", (lambda e, fi=fi: e.tensor_scalar(
                xb[:, fi, :], xb[:, fi, :], s_bias[:, fi:fi + 1], None, ALU.add)), "VE")

    def emit_klr(c, units=None):
        t0 = c * CH

        def batch(b):
            W("tn", "VE", G["proj_ve"])
            if b == 0:
                W("tn", "VE", G["xh_ready"])
            for fi in range(NF):
                O("tn", (lambda e, b=b, fi=fi: e.matmul(
                    p_proj[:, 0:CH], s_B[:, fi, b, :],
                    s_xh[:, fi, :], start=(fi == 0), stop=(fi == NF - 1))), None)
            mm = O("tn", (lambda e: e.matmul(
                p_proj[:, 0:CH], s_emat[:, :], s_one4[:, :],
                start=False, stop=True, skip_group_check=True)), "TN")
            W("ve", "TN", mm)
            G["proj_ve"] = O("ve", (lambda e, b=b, t0=t0: e.tensor_copy(
                s_klr[:, b, t0:t0 + CH], p_proj[:, 0:CH])), "VE")

        def norms():
            W("ve", "TN", G["sq_free"])
            ksq = O("ve", (lambda e, t0=t0: e.tensor_mul(
                s_sq[:, :, :], s_klr[:, :, t0:t0 + CH], s_klr[:, :, t0:t0 + CH])), "VE")
            kn_mm = 0
            for b in range(NB):
                W("tn", "VE", ksq if b == 0 else G["misc_ve"])
                if b == 0:
                    W("tn", "VE", G["misc_ve"])
                kn_mm = O("tn", (lambda e, b=b: e.matmul(
                    p_misc[0:NB, 0:CH], s_sel[:, :], s_sq[:, b, :],
                    start=True, stop=True, skip_group_check=True)), "TN")
                W("ve", "TN", kn_mm)
                O("ve", (lambda e, b=b: e.tensor_reduce(
                    s_cc[:, b:b + 1], p_misc[0:NB, 0:CH], AX.X, ALU.max)), "VE")
                G["misc_ve"] = O("ve", (lambda e, b=b: e.tensor_max(
                    s_kn[:, b:b + 1], s_kn[:, b:b + 1], s_cc[:, b:b + 1])), "VE")
            G["sq_free"] = kn_mm

        todo = [(lambda b=b: batch(b)) for b in range(NB)] + [norms]
        if units is None:
            for u in todo:
                u()
        else:
            units.extend(todo)

    def emit_v(c, units=None):
        def vunit(j, h2):
                kt = 2 * c + j
                W("tn", "VE", G["proj_ve"])
                for fi in range(NF):
                    mv = O("tn", (lambda e, j=j, h2=h2, fi=fi: e.matmul(
                        p_proj[:, :], s_xh[:, fi, j * KT:(j + 1) * KT],
                        s_wv[:, fi, h2 * 512:(h2 + 1) * 512],
                        start=(fi == 0), stop=(fi == NF - 1))),
                        "TN" if fi == NF - 1 else None)
                    if mv is not None:
                        mm = mv
                W("ve", "TN", mm)
                G["proj_ve"] = O("ve", (lambda e, kt=kt, h2=h2: e.tensor_copy(
                    v_s[:, kt, h2 * 8:(h2 + 1) * 8, 0:DH],
                    p_proj[:, :].rearrange("p (h d) -> p h d", d=DH))), "VE")
                G["v_ready"] = G["proj_ve"]

        todo = [(lambda j=j, h2=h2: vunit(j, h2)) for j in range(2) for h2 in range(2)]
        if units is None:
            for u in todo:
                u()
        else:
            units.extend(todo)

    def emit_qlr(c, units=None):
        def batch(b):
            W("tn", "VE", G["proj_ve"])
            if b == 0:
                W("tn", "VE", G["xh_ready"])
            for fi in range(NF):
                mv = O("tn", (lambda e, b=b, fi=fi: e.matmul(
                    p_proj[:, 0:CH], s_A[:, fi, b, :],
                    s_xh[:, fi, :], start=(fi == 0), stop=(fi == NF - 1))),
                    "TN" if fi == NF - 1 else None)
                if mv is not None:
                    mm = mv
            W("ve", "TN", mm)
            if b == 0:
                W("ve", "TN", G["qlr_free"])
            O("ve", (lambda e, b=b: e.tensor_copy(
                s_qlr[:, b, :], p_proj[:, 0:CH])), "VE")
            W("ve", "TN", G["sq_free"])
            qsq = O("ve", (lambda e, b=b: e.tensor_mul(
                s_sq[:, b, :], s_qlr[:, b, :], s_qlr[:, b, :])), "VE")
            W("tn", "VE", qsq)
            W("tn", "VE", G["misc_ve"])
            qn_mm = O("tn", (lambda e, b=b: e.matmul(
                p_misc[0:NB, 0:CH], s_sel[:, :], s_sq[:, b, :],
                start=True, stop=True, skip_group_check=True)), "TN")
            G["sq_free"] = qn_mm
            W("ve", "TN", qn_mm)
            qn_done = O("ve", (lambda e, b=b: e.tensor_reduce(
                s_qn[:, b:b + 1], p_misc[0:NB, 0:CH], AX.X, ALU.max)), "VE")
            G["misc_ve"] = qn_done
            cc1 = O("ve", (lambda e, b=b: e.tensor_mul(
                s_cc[:, b:b + 1], s_kn[:, b:b + 1], s_qn[:, b:b + 1])), "VE")
            W("sc", "VE", cc1)
            cc2 = O("sc", (lambda e, b=b: e.activation(
                s_cc[:, b:b + 1], s_cc[:, b:b + 1], AF.Ln)), "SC")
            W("ve", "SC", cc2)
            O("ve", (lambda e, b=b: e.tensor_scalar(
                s_cc[:, b:b + 1], s_cc[:, b:b + 1], 0.5, 0.01, ALU.mult, ALU.add)), "VE")
            cc3 = O("ve", (lambda e, b=b: e.tensor_scalar_min(
                s_cc[:, b:b + 1], s_cc[:, b:b + 1], 4.0943446)), "VE")
            W("sc", "VE", cc3)
            cc4 = O("sc", (lambda e, b=b: e.activation(
                s_cc[:, b:b + 1], s_cc[:, b:b + 1], AF.Exp)), "SC")
            W("ve", "SC", cc4)
            rows = O("ve", (lambda e, b=b: e.tensor_scalar(
                s_negc[:, :], s_one4[:, :], s_cc[0:NB, b:b + 1], -1.0,
                ALU.mult, ALU.mult)), "VE")
            W("tn", "VE", rows)
            inj = O("tn", (lambda e: e.matmul(
                p_proj[:, 0:CH], s_emat[:, :], s_negc[:, :],
                start=False, stop=True, skip_group_check=True)), "TN")
            W("ve", "TN", inj)
            ev = O("ve", (lambda e, b=b: e.tensor_copy(
                s_qlr[:, b, :], p_proj[:, 0:CH])), "VE")
            G["proj_ve"] = ev
            G["qlr_ready"] = ev

        todo = [(lambda b=b: batch(b)) for b in range(NB)]
        if units is None:
            for u in todo:
                u()
        else:
            units.extend(todo)

    def emit_ffn(c, units=None):
        buf = c % 2
        mlld = [0] * NF

        def prefetch():
            for fo in range(2):
                W("sync", "TN", G["mlp_tn"][fo % 2])
                mlld[fo] = DMA("sync", s_mlp[:, fo % 2, :, :], ml_sp2[fo, :, :, :], "IN")

        def founit(fo):
            W("tn", "VE", G["proj_ve"])
            if fo == 0:
                W("tn", "VE", G["xh_ready"])
            W("tn", "IN", mlld[fo])
            for ti, (xt, mt) in enumerate(((s_xh, None), (s_xl, None), (s_xh, None))):
                for fi in range(NF):
                    if ti < 2:
                        mv = O("tn", (lambda e, fi=fi, fo=fo, xt=xt,
                                      st=(ti == 0 and fi == 0): e.matmul(
                            p_proj[:, 0:CH], s_mhb[:, fi, fo * 128:(fo + 1) * 128],
                            xt[:, fi, :], start=st, stop=False)), None)
                    else:
                        mv = O("tn", (lambda e, fi=fi, fo=fo: e.matmul(
                            p_proj[:, 0:CH], s_mlp[:, fo % 2, fi, :],
                            s_xh[:, fi, :], start=False, stop=(fi == NF - 1))),
                            "TN" if fi == NF - 1 else None)
                    if mv is not None:
                        mm = mv
            G["mlp_tn"][fo % 2] = mm
            if fo + 2 < NF:
                W("sync", "TN", mm)
                mlld[fo + 2] = DMA("sync", s_mlp[:, fo % 2, :, :], ml_sp2[fo + 2, :, :, :], "IN")
            W("ve", "TN", mm)
            W("ve", "SP", G["x_free"][buf])
            G["proj_ve"] = O("ve", (lambda e, fo=fo, buf=buf: e.scalar_tensor_tensor(
                s_x[buf][:, fo, :], p_proj[:, 0:CH], 1.0,
                s_x[buf][:, fo, :], ALU.mult, ALU.add)), "VE")
            if fo == NF - 1:
                G["xh_free"] = mm

        todo = [prefetch] + [(lambda fo=fo: founit(fo)) for fo in range(NF)]
        if units is None:
            for u in todo:
                u()
        else:
            units.extend(todo)

    def emit_attention(c, units_mid, units_tail):
        ntiles = 2 * (c + 1)
        sc_snap = 0
        n_iters = 2 * NB * ntiles
        per = max(1, (n_iters + max(1, len(units_mid)) - 1) // max(1, len(units_mid)))
        uidx = [0]

        def pop_unit():
            if uidx[0] < len(units_mid):
                units_mid[uidx[0]]()
                uidx[0] += 1

        it = [0]
        for s in range(2 * NB):
            b, gh = s // 2, s % 2
            sp = s % 2
            h0 = 4 * b + 2 * gh
            pend = None

            def emit_attnv(t, par, maskd, e0, e1, last):
                if maskd is not None:
                    W("tn", "VE", maskd)
                if t == 0:
                    W("tn", "VE", G["att_ve"][sp])
                j = t - 2 * c
                c0 = 128 if j == 1 else 0
                for gl in range(2):
                    W("tn", "SC", e0 if gl == 0 else e1)
                    av = O("tn", (lambda e, gl=gl, h=h0 + gl, t=t, par=par, sp=sp, c0=c0,
                                  st=(t == 0 and gl == 0),
                                  sp_=(last and gl == 1): e.matmul(
                        p_att[:, sp, gl, c0:CH], v_s[:, t, h, :],
                        s_ring[:, par, gl, c0:CH],
                        start=st, stop=sp_, skip_group_check=True)),
                        "TN" if gl == 1 else None)
                G["ring_tn"][par] = av
                return av

            for t in range(ntiles):
                par = t % 2
                W("tn", "SC", G["exp_snap"][par])
                if t == 0 and s == 0:
                    W("tn", "VE", G["qlr_ready"])
                    W("tn", "VE", G["v_ready"])
                for gl in range(2):
                    g = 2 * gh + gl
                    mv = O("tn", (lambda e, g=g, b=b, t=t, par=par, gl=gl: e.matmul(
                        p_s[:, 2 * par + gl, 0:CH],
                        s_klr[32 * g:32 * g + 17, b, t * KT:(t + 1) * KT],
                        s_qlr[32 * g:32 * g + 17, b, :],
                        start=True, stop=True, skip_group_check=True,
                        tile_position=(32 * g, 0))), "TN")
                sc_snap = mv
                W("sc", "TN", sc_snap)
                W("sc", "TN", G["ring_tn"][par])
                expd0 = O("sc", (lambda e, par=par: e.activation(
                    s_ring[:, par, 0, :], p_s[:, 2 * par, 0:CH],
                    AF.Exp)), "SC")
                expd = O("sc", (lambda e, par=par: e.activation(
                    s_ring[:, par, 1, :], p_s[:, 2 * par + 1, 0:CH],
                    AF.Exp)), "SC")
                G["ring_sc"][par] = expd
                G["exp_snap"][par] = expd
                maskd = None
                if t == 2 * c:
                    W("ve", "SC", expd)
                    for gl in range(2):
                        maskd = O("ve", (lambda e, par=par, gl=gl: e.tensor_mul(
                            s_ring[:, par, gl, :], s_ring[:, par, gl, :],
                            s_mask[:, :])), "VE")
                elif t == 2 * c + 1:
                    W("ve", "SC", expd)
                    for gl in range(2):
                        maskd = O("ve", (lambda e, par=par, gl=gl: e.tensor_mul(
                            s_ring[:, par, gl, 128:256], s_ring[:, par, gl, 128:256],
                            s_mask[:, 0:128])), "VE")
                if pend is not None:
                    emit_attnv(*pend, last=False)
                pend = (t, par, maskd, expd0, expd)
                it[0] += 1
                if it[0] % per == 0:
                    pop_unit()
            att_mm = emit_attnv(*pend, last=True)
            G["qlr_free"] = sc_snap
            W("ve", "TN", att_mm)
            W("ve", "TN", G["scr_tn"])
            O("ve", (lambda e, sp=sp: e.tensor_copy(
                s_att[:, :, :], p_att[0:64, sp, :, :])), "VE")
            O("ve", (lambda e, sp=sp: e.tensor_copy(
                s_scr[0:1, 0:2, :], p_att[64:65, sp, :, :])), "VE")
            W("tn", "VE", S.cnt["VE"])
            W("tn", "VE", G["misc_ve"])
            bcm = O("tn", (lambda e: e.matmul(
                p_misc[:, :], s_onesr[:, :],
                s_scr[0:1, 0:2, :].rearrange("p b t -> p (b t)"),
                start=True, stop=True, skip_group_check=True)), "TN")
            G["scr_tn"] = bcm
            W("ve", "TN", bcm)
            O("ve", (lambda e: e.reciprocal(p_misc[:, :], p_misc[:, :])), "VE")
            if s == 0:
                W("ve", "TN", G["afm_tn"])
            O("ve", (lambda e, s=s: e.tensor_mul(
                s_afm[0:64, s, :], s_att[:, 0, :], p_misc[0:64, 0:CH])), "VE")
            afm_done = O("ve", (lambda e, s=s: e.tensor_mul(
                s_afm[64:128, s, :], s_att[:, 1, :], p_misc[64:128, CH:2 * CH])), "VE")
            G["misc_ve"] = afm_done
            G["att_ve"][sp] = afm_done
        while uidx[0] < len(units_mid):
            pop_unit()
        for u in units_tail:
            u()
        return afm_done

    def emit_wo_out(c, afm_done):
        buf = c % 2
        t0 = c * CH
        out_ve = 0
        for fo in range(NF):
            W("tn", "VE", G["proj_ve"])
            if fo == 0:
                W("tn", "VE", afm_done)
            for fi in range(NF):
                mm = O("tn", (lambda e, fi=fi, fo=fo: e.matmul(
                    p_proj[:, 0:CH], s_wo[:, fi, fo * 128:(fo + 1) * 128],
                    s_afm[:, fi, :], start=(fi == 0), stop=(fi == NF - 1))),
                    "TN" if fi == NF - 1 else None)
            if fo == NF - 1:
                G["afm_tn"] = mm
            W("ve", "TN", mm)
            out_ve = O("ve", (lambda e, fo=fo, buf=buf: e.scalar_tensor_tensor(
                s_x[buf][:, fo, :], p_proj[:, 0:CH], 1.0,
                s_x[buf][:, fo, :], ALU.mult, ALU.add)), "VE")
            G["proj_ve"] = out_ve
        W("gp", "VE", out_ve)
        G["x_free"][buf] = DMA8("gp", lambda i, t0=t0: out_fm[i, :, t0:t0 + CH],
                                lambda i, buf=buf: s_x[buf][:, i, :], "SP")

    def emit_proj(c):
        emit_ln(c)
        emit_klr(c)
        emit_v(c)
        emit_ffn(c)
        emit_qlr(c)

    emit_proj(0)

    for c in range(NCH):
        units_mid, units_tail = [], []
        if c + 1 < NCH:
            cn = c + 1
            units_mid.append(lambda cn=cn: emit_ln(cn))
            emit_klr(cn, units_mid)
            emit_v(cn, units_mid)
            emit_ffn(cn, units_mid)
            emit_qlr(cn, units_tail)
        afm_done = emit_attention(c, units_mid, units_tail)
        emit_wo_out(c, afm_done)
        if c + 2 < NCH:
            cc = c + 2
            W("sync", "SP", G["x_free"][cc % 2])
            x_snap[cc] = DMA8("sync", lambda i, cc=cc: s_x[cc % 2][:, i, :],
                              lambda i, cc=cc: x_fm[i, :, cc * CH:(cc + 1) * CH], "IN")

    S.wait("gp", "SP")

    def replay(eng):
        def run(e):
            for item in S.ops[eng]:
                if item[0] == "w":
                    e.wait_ge(sems[item[1]], item[2])
                else:
                    instr = item[1](e)
                    if item[2] is not None:
                        instr.then_inc(sems[item[2]], item[3])
        return run

    with nc.Block() as block:
        block.sync(replay("sync"))
        block.gpsimd(replay("gp"))
        block.vector(replay("ve"))
        block.scalar(replay("sc"))
        block.tensor(replay("tn"))

    top.close()
    return nc


_NC_CACHE = None


def _get_program():
    global _NC_CACHE
    if _NC_CACHE is None:
        _NC_CACHE = build_program()
    return _NC_CACHE


def kernel(L0_attn_x, L0_attn_W_qkv, L0_attn_W_q_lsr, L0_attn_W_k_lsr, L0_attn_core,
           L0_attn_W_o, L0_ffn_w_freq_real, L0_ffn_w_freq_imag, L0_ffn_bias):
    x = np.asarray(L0_attn_x, np.float32)
    W_qkv = np.ascontiguousarray(np.asarray(L0_attn_W_qkv, np.float32))
    W_qlsr = np.asarray(L0_attn_W_q_lsr, np.float32)
    W_klsr = np.asarray(L0_attn_W_k_lsr, np.float32)
    core = np.asarray(L0_attn_core, np.float32)
    W_o_np = np.ascontiguousarray(np.asarray(L0_attn_W_o, np.float32))
    Wr = np.ascontiguousarray(np.asarray(L0_ffn_w_freq_real, np.float32))
    Wi = np.ascontiguousarray(np.asarray(L0_ffn_w_freq_imag, np.float32))
    biasn = np.asarray(L0_ffn_bias, np.float32)

    jk = np.outer(np.arange(D, dtype=np.float64), np.arange(D, dtype=np.float64))
    ang = 2.0 * np.pi * (jk % D) / D
    C_np = np.ascontiguousarray(np.cos(ang).astype(np.float32))
    S_np = np.ascontiguousarray(np.sin(ang).astype(np.float32))
    masks_np = np.zeros((128, CH), BFNP)
    for kk in range(128):
        masks_np[kk, :] = (np.arange(CH) >= kk).astype(BFNP)
    sel_np = np.zeros((128, NB), np.float16)
    for g in range(NB):
        for r in range(16):
            sel_np[32 * g + r, g] = 1.0
    e_np = np.zeros((NB, 128), np.float16)
    for g in range(NB):
        e_np[g, 32 * g + 16] = 1.0

    shared = {
        "wq_t": np.ascontiguousarray(
            W_qkv[:, 0:D].reshape(D, H, DH).transpose(2, 1, 0)),
        "wk_t": np.ascontiguousarray(
            W_qkv[:, D:2 * D].reshape(D, H, DH).transpose(2, 1, 0)),
        "w_v": np.ascontiguousarray(W_qkv[:, 2 * D:3 * D]),
        "qlsr_t": np.ascontiguousarray(W_qlsr.transpose(1, 0, 2)),
        "klsr_t": np.ascontiguousarray(W_klsr.transpose(1, 0, 2)),
        "core_r": np.ascontiguousarray(core.reshape(1, H * R)),
        "w_o": W_o_np, "w_fr": Wr, "w_fi": Wi,
        "biasv": np.ascontiguousarray(biasn.reshape(NF, 128).T),
        "cmat": C_np, "smat": S_np, "cneg": np.ascontiguousarray(-C_np),
        "masks": masks_np, "selmat": sel_np, "e_mat": e_np,
        "onesrow": np.ones((1, 128), np.float32),
        "onesb": np.full((128, 1), 1.0 / D, np.float16),
    }
    in_maps = []
    for c in range(NC):
        m = dict(shared)
        m["x_fm"] = np.ascontiguousarray(x[c].T).reshape(NF, 128, T)
        in_maps.append(m)

    res = run_bass_kernel_spmd(_get_program(), in_maps, core_ids=list(range(NC)))
    outs = []
    for c in range(NC):
        o = np.asarray(res.results[c]["out_fm"]).reshape(D, T)
        outs.append(np.ascontiguousarray(o.T))
    return np.stack(outs).astype(np.float32)


# version 24
# speedup vs baseline: 1.1121x; 1.1121x over previous
import numpy as np
import ml_dtypes
import concourse.bass as bass
import concourse.mybir as mybir
from concourse.bass_utils import run_bass_kernel_spmd
from contextlib import ExitStack

F32 = mybir.dt.float32
F32R = mybir.dt.float32r
BF16 = mybir.dt.bfloat16
AF = mybir.ActivationFunctionType
ALU = mybir.AluOpType
AX = mybir.AxisListType
BFNP = ml_dtypes.bfloat16

NC = 8
D = 1024
T = 4096
H = 16
DH = 64
NP = 8
NF = 8
CH = 256
NCH = T // CH
KT = 128
KPC = CH // KT
NKT = T // KT
EPS = 1e-5

ENGS = ("sync", "gp", "ve", "sc", "tn")
SEM_NAMES = ("IN", "SP", "TN", "VE", "SC")


class Sched:
    def __init__(self):
        self.ops = {e: [] for e in ENGS}
        self.cnt = {s: 0 for s in SEM_NAMES}

    def wait(self, eng, sem, val=None):
        v = self.cnt[sem] if val is None else val
        if v > 0:
            self.ops[eng].append(("w", sem, v))

    def op(self, eng, fn, sem=None, amt=1):
        if sem is not None:
            self.cnt[sem] += amt
            self.ops[eng].append(("i", fn, sem, amt))
            return self.cnt[sem]
        self.ops[eng].append(("i", fn, None, 0))
        return None

    def dma(self, eng, out, in_, sem):
        return self.op(eng, (lambda e, o=out, i=in_: e.dma_start(out=o, in_=i)), sem, 16)

    def barrier(self):
        snap = dict(self.cnt)
        for e in ENGS:
            for s in SEM_NAMES:
                self.wait(e, s, snap[s])


def build_program():
    nc = bass.Bass("TRN2", target_bir_lowering=False, debug=False)

    x_fm = nc.dram_tensor("x_fm", [NF, 128, T], F32, kind="ExternalInput").ap()
    w_kv = nc.dram_tensor("w_kv", [D, 2 * D], F32, kind="ExternalInput").ap()
    wqT = nc.dram_tensor("wqT", [128, NP, D], F32, kind="ExternalInput").ap()
    qlsrT = nc.dram_tensor("qlsrT", [16, H, DH], F32, kind="ExternalInput").ap()
    klsrT = nc.dram_tensor("klsrT", [16, H, DH], F32, kind="ExternalInput").ap()
    coreT = nc.dram_tensor("coreT", [16, H], F32, kind="ExternalInput").ap()
    w_o = nc.dram_tensor("w_o", [D, D], F32, kind="ExternalInput").ap()
    w_fr = nc.dram_tensor("w_fr", [D, D], F32, kind="ExternalInput").ap()
    w_fi = nc.dram_tensor("w_fi", [D, D], F32, kind="ExternalInput").ap()
    biasv = nc.dram_tensor("biasv", [128, NF], F32, kind="ExternalInput").ap()
    cmat = nc.dram_tensor("cmat", [D, D], F32, kind="ExternalInput").ap()
    smat = nc.dram_tensor("smat", [D, D], F32, kind="ExternalInput").ap()
    masks = nc.dram_tensor("masks", [KPC, 128, CH], BF16, kind="ExternalInput").ap()
    selcol = nc.dram_tensor("selcol", [65, H, H], BF16, kind="ExternalInput").ap()
    selbc = nc.dram_tensor("selbc", [H, NP, 128], F32, kind="ExternalInput").ap()
    onescol = nc.dram_tensor("onescol", [128, 1], F32, kind="ExternalInput").ap()
    ones128 = nc.dram_tensor("ones128", [128, 1], F32, kind="ExternalInput").ap()
    onesrow = nc.dram_tensor("onesrow", [1, 128], F32, kind="ExternalInput").ap()
    cneg = nc.dram_tensor("cneg", [D, D], F32, kind="ExternalInput").ap()
    onesbf = nc.dram_tensor("onesbf", [128, 1], BF16, kind="ExternalInput").ap()
    ones1b = nc.dram_tensor("ones1b", [128, 1], BF16, kind="ExternalInput").ap()

    out_fm = nc.dram_tensor("out_fm", [NF, 128, T], F32, kind="ExternalOutput").ap()

    wtq_sp = nc.dram_tensor("wtq_sp", [NF, 128, D], BF16, kind="Internal").ap()
    at_sp = nc.dram_tensor("at_sp", [NF, 128, D], F32, kind="Internal").ap()
    nbt_sp = nc.dram_tensor("nbt_sp", [NF, 128, D], F32, kind="Internal").ap()
    mh_sp = nc.dram_tensor("mh_sp", [NF, 128, D], BF16, kind="Internal").ap()
    ml_sp = nc.dram_tensor("ml_sp", [NF, 128, D], BF16, kind="Internal").ap()
    qt_sp = nc.dram_tensor("qt_sp", [NP, 128, T], BF16, kind="Internal").ap()
    xh_sp = nc.dram_tensor("xh_sp", [NF, 128, T], BF16, kind="Internal").ap()
    xl_sp = nc.dram_tensor("xl_sp", [NF, 128, T], BF16, kind="Internal").ap()
    part_sp = nc.dram_tensor("part_sp", [NF, 128, T], F32, kind="Internal").ap()

    top = ExitStack()
    sems = {s: top.enter_context(nc.semaphore(s)) for s in SEM_NAMES}
    S = Sched()
    W, O, DMA = S.wait, S.op, S.dma

    s_ones = top.enter_context(nc.sbuf_tensor("s_ones", [128, 1], F32))
    s_ones1 = top.enter_context(nc.sbuf_tensor("s_ones1", [128, 1], F32))
    s_onesr = top.enter_context(nc.sbuf_tensor("s_onesr", [1, 128], F32))
    s_onesb = top.enter_context(nc.sbuf_tensor("s_onesb", [128, 1], BF16))
    s_ones1b = top.enter_context(nc.sbuf_tensor("s_ones1b", [128, 1], BF16))

    DMA("sync", s_ones[:], onescol[:, :], "IN")
    DMA("sync", s_ones1[:], ones128[:, :], "IN")
    DMA("sync", s_onesr[:], onesrow[:, :], "IN")
    DMA("sync", s_onesb[:], onesbf[:, :], "IN")
    DMA("sync", s_ones1b[:], ones1b[:, :], "IN")

    with ExitStack() as ph:
        s_wqT = ph.enter_context(nc.sbuf_tensor("s_wqT", [128, NP, D], F32))
        s_qlsr = ph.enter_context(nc.sbuf_tensor("s_qlsr", [16, H, DH], F32))
        s_klsr = ph.enter_context(nc.sbuf_tensor("s_klsr", [16, H, DH], F32))
        s_coreT = ph.enter_context(nc.sbuf_tensor("s_coreT", [16, H], F32))
        s_qsc = ph.enter_context(nc.sbuf_tensor("s_qsc", [16, H, DH], F32))
        s_G = ph.enter_context(nc.sbuf_tensor("s_G", [128, NP, DH], F32))
        s_wtq = ph.enter_context(nc.sbuf_tensor("s_wtq0", [128, NF, D], BF16))
        p_G = ph.enter_context(nc.psum_tensor("p_G", [DH, H, DH], F32))
        p_wq = [ph.enter_context(nc.psum_tensor(f"p_wq{i}", [128, NF, DH], F32)) for i in range(2)]

        DMA("sync", s_wqT[:], wqT[:, :, :], "IN")
        DMA("sync", s_qlsr[:], qlsrT[:, :, :], "IN")
        DMA("sync", s_klsr[:], klsrT[:, :, :], "IN")
        DMA("sync", s_coreT[:], coreT[:, :], "IN")
        W("ve", "IN")
        for h in range(H):
            qsc_done = O("ve", (lambda e, h=h: e.tensor_scalar(
                s_qsc[:, h, :], s_qlsr[:, h, :], s_coreT[:, h:h + 1], 0.25,
                ALU.mult, ALU.mult)), "VE")
        W("tn", "VE", qsc_done)
        for h in range(H):
            g_done = O("tn", (lambda e, h=h: e.matmul(
                p_G[:, h, :], s_qsc[:, h, :], s_klsr[:, h, :],
                start=(h % 8 == 0), stop=True, skip_group_check=True)), "TN")
        W("ve", "TN", g_done)
        O("ve", (lambda e: e.tensor_copy(s_G[0:64, :, :], p_G[:, 0:H:2, :])), "VE")
        sg_done = O("ve", (lambda e: e.tensor_copy(s_G[64:128, :, :], p_G[:, 1:H:2, :])), "VE")
        W("tn", "VE", sg_done)
        wq_evac = [0, 0]
        for h in range(H):
            b = h % 2
            hp = h % 2
            pri = h // 2
            W("tn", "VE", wq_evac[b])
            for fi in range(NF):
                t_done = O("tn", (lambda e, hp=hp, pri=pri, fi=fi, b=b: e.matmul(
                    p_wq[b][:, fi, :],
                    s_wqT[hp * 64:(hp + 1) * 64, pri, fi * 128:(fi + 1) * 128],
                    s_G[hp * 64:(hp + 1) * 64, pri, :],
                    start=(fi == 0), stop=(fi == NF - 1), skip_group_check=True)), "TN")
            W("ve", "TN", t_done)
            wq_evac[b] = O("ve", (lambda e, h=h, b=b: e.tensor_copy(
                s_wtq[:, 0:NF, h * DH:(h + 1) * DH], p_wq[b][:, :, :])), "VE")
        W("gp", "VE")
        DMA("gp", wtq_sp.rearrange("i p n -> p i n"), s_wtq[:], "SP")
        S.barrier()

    with ExitStack() as ph:
        s_C = ph.enter_context(nc.sbuf_tensor("s_C", [128, NF, D], F32R))
        s_Sm = ph.enter_context(nc.sbuf_tensor("s_Sm", [128, NF, D], F32R))
        p_st = [ph.enter_context(nc.psum_tensor(f"p_st{i}", [128, 512], F32)) for i in range(3)]
        evsnap = [0, 0, 0]
        bank = [0]

        def stage_mm(terms, dst_sp, s_tmp, scale=None, mh=None):
            tsp_free = [0, 0]
            for oc in range(NF):
                for wi in range(2):
                    b = bank[0] % 3
                    bank[0] += 1
                    W("tn", "VE", evsnap[b])
                    nterm = len(terms)
                    for ti, (lt, rt) in enumerate(terms):
                        for dc in range(NF):
                            t_done = O("tn", (lambda e, lt=lt, rt=rt, dc=dc, oc=oc, wi=wi, b=b,
                                              st=(ti == 0 and dc == 0),
                                              sp=(ti == nterm - 1 and dc == NF - 1): e.matmul(
                                p_st[b][:, :],
                                lt[:, dc, oc * 128:(oc + 1) * 128],
                                rt[:, dc, wi * 512:(wi + 1) * 512],
                                start=st, stop=sp)), "TN")
                    W("ve", "TN", t_done)
                    tb = wi
                    W("ve", "SP", tsp_free[tb])
                    if scale is None:
                        ev = O("ve", (lambda e, b=b, tb=tb: e.tensor_copy(
                            s_tmp[tb][:, :], p_st[b][:, :])), "VE")
                    else:
                        ev = O("ve", (lambda e, b=b, tb=tb: e.tensor_scalar_mul(
                            s_tmp[tb][:, :], p_st[b][:, :], scale)), "VE")
                    if mh is not None:
                        mh_sp_, ml_sp_, s_mh, s_ml = mh
                        O("ve", (lambda e, tb=tb, s_mh=s_mh, s_tmp=s_tmp: e.tensor_copy(
                            s_mh[tb][:, :], s_tmp[tb][:, :])), "VE")
                        ev = O("ve", (lambda e, tb=tb, s_mh=s_mh, s_ml=s_ml, s_tmp=s_tmp: e.tensor_sub(
                            s_ml[tb][:, :], s_tmp[tb][:, :], s_mh[tb][:, :])), "VE")
                    evsnap[b] = ev
                    W("gp", "VE", ev)
                    if mh is None:
                        tsp_free[tb] = DMA("gp", dst_sp[oc, :, wi * 512:(wi + 1) * 512],
                                           s_tmp[tb][:], "SP")
                    else:
                        mh_sp_, ml_sp_, s_mh, s_ml = mh
                        DMA("gp", mh_sp_[oc, :, wi * 512:(wi + 1) * 512], s_mh[tb][:], "SP")
                        tsp_free[tb] = DMA("gp", ml_sp_[oc, :, wi * 512:(wi + 1) * 512],
                                           s_ml[tb][:], "SP")

        with ExitStack() as ph1:
            s_Wr = ph1.enter_context(nc.sbuf_tensor("s_Wr", [128, NF, D], F32R))
            s_Wi = ph1.enter_context(nc.sbuf_tensor("s_Wi", [128, NF, D], F32R))
            s_tmp = [ph1.enter_context(nc.sbuf_tensor(f"s_tmp{i}", [128, 512], F32)) for i in range(2)]
            for src, dst in ((cmat, s_C), (smat, s_Sm), (w_fr, s_Wr), (w_fi, s_Wi)):
                DMA("gp", dst[:], src.rearrange("(i p) n -> p i n", p=128), "SP")
            W("tn", "SP")
            stage_mm([(s_Wr, s_C), (s_Wi, s_Sm)], at_sp, s_tmp)
            W("gp", "TN")
            cneg_snap = DMA("gp", s_C[:], cneg.rearrange("(i p) n -> p i n", p=128), "SP")
            W("tn", "SP", cneg_snap)
            stage_mm([(s_Wr, s_Sm), (s_Wi, s_C)], nbt_sp, s_tmp)
            S.barrier()
        with ExitStack() as ph2:
            s_AT = ph2.enter_context(nc.sbuf_tensor("s_AT", [128, NF, D], F32R))
            s_NBT = ph2.enter_context(nc.sbuf_tensor("s_NBT", [128, NF, D], F32R))
            s_tmp2 = [ph2.enter_context(nc.sbuf_tensor(f"s_tm2{i}", [128, 512], F32)) for i in range(2)]
            s_mh0 = [ph2.enter_context(nc.sbuf_tensor(f"s_mh{i}", [128, 512], BF16)) for i in range(2)]
            s_ml0 = [ph2.enter_context(nc.sbuf_tensor(f"s_ml{i}", [128, 512], BF16)) for i in range(2)]
            c2_snap = DMA("gp", s_C[:], cmat.rearrange("(i p) n -> p i n", p=128), "SP")
            DMA("gp", s_AT[:], at_sp.rearrange("i p n -> p i n"), "SP")
            at_snap = DMA("gp", s_NBT[:], nbt_sp.rearrange("i p n -> p i n"), "SP")
            W("tn", "SP", at_snap)
            stage_mm([(s_AT, s_C), (s_NBT, s_Sm)], None, s_tmp2, scale=1.0 / D,
                     mh=(mh_sp, ml_sp, s_mh0, s_ml0))
            S.barrier()

    k_s = top.enter_context(nc.sbuf_tensor("k_s", [128, NP, T], BF16))
    v_s = top.enter_context(nc.sbuf_tensor("v_s", [128, NKT, H, 65], BF16))
    s_negc = top.enter_context(nc.sbuf_tensor("s_negc", [128, NP], F32))

    with ExitStack() as ph1:
        s_x = [ph1.enter_context(nc.sbuf_tensor(f"s_x{i}", [128, NF, CH], F32)) for i in range(2)]

        x_snap = [0] * NCH
        xfree_snap = [0] * NCH

        def emit_ln(c, buf, out_bf=None):
            xb = s_x[buf]
            W("ve", "IN", x_snap[c])
            xbf_done = O("ve", (lambda e: e.tensor_copy(s_xbf[:], xb[:])), "VE")
            W("tn", "VE", xbf_done)
            for fi in range(NF):
                mean_done = O("tn", (lambda e, fi=fi: e.matmul(
                    p_stat[:, 0, :], s_onesb[:, :], s_xbf[:, fi, :],
                    start=(fi == 0), stop=(fi == NF - 1), skip_group_check=True)), "TN")
            W("sc", "VE", xbf_done)
            W("sc", "TN", mean_done)
            sq_done = O("sc", (lambda e: e.activation(s_xbf[:], s_xbf[:], AF.Square)), "SC")
            W("tn", "SC", sq_done)
            for fi in range(NF):
                m2_done = O("tn", (lambda e, fi=fi: e.matmul(
                    p_stat[:, 1, :], s_onesb[:, :], s_xbf[:, fi, :],
                    start=False, stop=(fi == NF - 1), skip_group_check=True)), "TN")
            W("ve", "TN", m2_done)
            O("ve", (lambda e: e.tensor_copy(s_stat[:, 0:2, :], p_stat[:, :, :])), "VE")
            O("ve", (lambda e: e.tensor_mul(s_stat[:, 2, :], s_stat[:, 0, :], s_stat[:, 0, :])), "VE")
            O("ve", (lambda e: e.tensor_sub(s_stat[:, 2, :], s_stat[:, 1, :], s_stat[:, 2, :])), "VE")
            var_done = O("ve", (lambda e: e.tensor_scalar_add(s_stat[:, 2, :], s_stat[:, 2, :], EPS)), "VE")
            W("sc", "VE", var_done)
            O("sc", (lambda e: e.activation(s_stat[:, 2, :], s_stat[:, 2, :], AF.Ln)), "SC")
            rstd_done = O("sc", (lambda e: e.activation(s_stat[:, 3, :], s_stat[:, 2, :], AF.Exp, scale=-0.5)), "SC")
            W("tn", "SC", rstd_done)
            O("tn", (lambda e: e.matmul(p_bcst[:, 0, :], s_onesr[:, :],
                                        s_stat[:, 0, :], start=True, stop=True,
                                        skip_group_check=True)), "TN")
            bc_done = O("tn", (lambda e: e.matmul(p_bcst[:, 1, :], s_onesr[:, :],
                                                  s_stat[:, 3, :], start=False, stop=True,
                                                  skip_group_check=True)), "TN")
            W("ve", "TN", bc_done)
            for fi in range(NF):
                O("ve", (lambda e, fi=fi: e.tensor_sub(s_xsq[:, fi, :], xb[:, fi, :], p_bcst[:, 0, :])), "VE")
                if out_bf is None:
                    xn_done = O("ve", (lambda e, fi=fi: e.tensor_mul(
                        s_xsq[:, fi, :], s_xsq[:, fi, :], p_bcst[:, 1, :])), "VE")
                else:
                    xn_done = O("ve", (lambda e, fi=fi: e.tensor_mul(
                        out_bf[:, fi, :], s_xsq[:, fi, :], p_bcst[:, 1, :])), "VE")
            return xn_done

        s_nmax = ph1.enter_context(nc.sbuf_tensor("s_nmax", [1, 2, NP], F32))
        s_ntmp = ph1.enter_context(nc.sbuf_tensor("s_ntmp", [1, 1], F32))
        O("ve", (lambda e: e.memset(s_nmax[:], 0.0)), "VE")

        def emit_norm(src_ap, which, pr, s_sq2, p_n2, st):
            b = st["i"] % 2
            st["i"] += 1
            W("sc", "TN", st["scg"][b])
            sq_done = O("sc", (lambda e, b=b, src_ap=src_ap, s_sq2=s_sq2: e.activation(
                s_sq2[:, b, :], src_ap, AF.Square)), "SC")
            W("tn", "SC", sq_done)
            W("tn", "VE", st["veg"][b])
            mm = O("tn", (lambda e, b=b, p_n2=p_n2, s_sq2=s_sq2: e.matmul(
                p_n2[b][:, :], s_ones1b[:, :], s_sq2[:, b, :],
                start=True, stop=True, skip_group_check=True)), "TN")
            st["scg"][b] = mm
            W("ve", "TN", mm)
            O("ve", (lambda e, b=b, p_n2=p_n2: e.tensor_reduce(
                s_ntmp[:, :], p_n2[b][:, :], AX.X, ALU.max)), "VE")
            st["veg"][b] = O("ve", (lambda e, pr=pr, which=which: e.tensor_max(
                s_nmax[:, which, pr:pr + 1], s_nmax[:, which, pr:pr + 1], s_ntmp[:, :])), "VE")

        with ExitStack() as ph:
            s_wkv = ph.enter_context(nc.sbuf_tensor("s_wkv", [128, NF, 2 * D], BF16))
            s_xsq = ph.enter_context(nc.sbuf_tensor("s_xsq", [128, NF, CH], F32))
            s_xbf = ph.enter_context(nc.sbuf_tensor("s_xbf", [128, NF, CH], BF16))
            s_stat = ph.enter_context(nc.sbuf_tensor("s_stat", [1, 4, CH], F32))
            p_stat = ph.enter_context(nc.psum_tensor("p_stat", [1, 2, CH], F32))
            p_bcst = ph.enter_context(nc.psum_tensor("p_bcst", [128, 2, CH], F32))
            s_xh = [ph.enter_context(nc.sbuf_tensor(f"s_xh{i}", [128, NF, CH], BF16)) for i in range(2)]
            s_xl = ph.enter_context(nc.sbuf_tensor("s_xl", [128, NF, CH], BF16))
            s_sq = ph.enter_context(nc.sbuf_tensor("s_sq", [128, 2, CH], BF16))
            p_k = [ph.enter_context(nc.psum_tensor(f"p_k{i}", [128, CH], F32)) for i in range(2)]
            p_v = [ph.enter_context(nc.psum_tensor(f"p_v{i}", [128, 512], F32)) for i in range(2)]
            p_n = [ph.enter_context(nc.psum_tensor(f"p_n{i}", [1, CH], F32)) for i in range(2)]
            nst = {"i": 0, "scg": [0, 0], "veg": [0, 0]}

            wkv_snap = DMA("gp", s_wkv[:], w_kv.rearrange("(i p) n -> p i n", p=128), "SP")
            vs_init = O("ve", (lambda e: e.memset(v_s[:], 1.0)), "VE")
            x_snap[0] = DMA("sync", s_x[0][:], x_fm.rearrange("i p t -> p i t")[:, :, 0:CH], "IN")

            pkg = [0, 0]
            pvg = [0, 0]
            xh_spilled = [0, 0]
            xl_spilled = 0

            for c in range(NCH):
                buf = c % 2
                t0 = c * CH
                if c + 1 < NCH:
                    if c >= 1:
                        W("sync", "VE", xfree_snap[c - 1])
                    x_snap[c + 1] = DMA("sync", s_x[1 - buf][:],
                                        x_fm.rearrange("i p t -> p i t")[:, :, t0 + CH:t0 + 2 * CH], "IN")
                xb2 = c % 2
                xn_done = emit_ln(c, buf, out_bf=None)
                xfree_snap[c] = xn_done
                if c >= 1:
                    W("ve", "SP", xh_spilled[1 - xb2])
                if c >= 1:
                    W("ve", "SP", xl_spilled)
                for fi in range(NF):
                    O("ve", (lambda e, fi=fi, xb2=xb2: e.tensor_copy(
                        s_xh[xb2][:, fi, :], s_xsq[:, fi, :])), "VE")
                    xh_done = O("ve", (lambda e, fi=fi, xb2=xb2: e.tensor_sub(
                        s_xl[:, fi, :], s_xsq[:, fi, :], s_xh[xb2][:, fi, :])), "VE")
                W("gp", "VE", xh_done)
                xh_spilled[xb2] = DMA("gp", xh_sp[:, :, t0:t0 + CH].rearrange("i p t -> p i t"),
                                      s_xh[xb2][:], "SP")
                xl_spilled = DMA("gp", xl_sp[:, :, t0:t0 + CH].rearrange("i p t -> p i t"),
                                 s_xl[:], "SP")
                W("tn", "VE", xh_done)
                if c == 0:
                    W("tn", "SP", wkv_snap)
                    W("tn", "VE", vs_init)
                for pr in range(NP):
                    b = pr % 2
                    W("tn", "VE", pkg[b])
                    for fi in range(NF):
                        k_done = O("tn", (lambda e, fi=fi, pr=pr, b=b, xb2=xb2: e.matmul(
                            p_k[b][:, :], s_wkv[:, fi, pr * 128:(pr + 1) * 128],
                            s_xh[xb2][:, fi, :], start=(fi == 0), stop=(fi == NF - 1))), "TN")
                    W("ve", "TN", k_done)
                    pkg[b] = O("ve", (lambda e, pr=pr, b=b, t0=t0: e.tensor_copy(
                        k_s[:, pr, t0:t0 + CH], p_k[b][:, :])), "VE")
                    W("sc", "VE", pkg[b])
                    emit_norm(k_s[:, pr, t0:t0 + CH], 0, pr, s_sq, p_n, nst)
                for j in range(KPC):
                    kt = c * KPC + j
                    for half in range(2):
                        W("tn", "VE", pvg[half])
                        for fi in range(NF):
                            v_done = O("tn", (lambda e, fi=fi, j=j, half=half, xb2=xb2: e.matmul(
                                p_v[half][:, :], s_xh[xb2][:, fi, j * KT:(j + 1) * KT],
                                s_wkv[:, fi, D + half * 512:D + (half + 1) * 512],
                                start=(fi == 0), stop=(fi == NF - 1))), "TN")
                        W("ve", "TN", v_done)
                        pvg[half] = O("ve", (lambda e, kt=kt, half=half: e.tensor_copy(
                            v_s[:, kt, half * 8:(half + 1) * 8, 0:DH],
                            p_v[half][:, :].rearrange("p (h d) -> p h d", d=DH))), "VE")
            S.barrier()

        with ExitStack() as ph:
            s_wtqq = ph.enter_context(nc.sbuf_tensor("s_wtqq1", [128, NF, D], BF16))
            s_xnq2 = [ph.enter_context(nc.sbuf_tensor(f"s_xnq2q{i}", [128, NF, CH], BF16)) for i in range(2)]
            s_qtile = [ph.enter_context(nc.sbuf_tensor(f"s_qtile{i}", [128, NP, CH], BF16)) for i in range(2)]
            s_sqq2 = ph.enter_context(nc.sbuf_tensor("s_sqq2q", [128, 2, CH], BF16))
            s_crow = ph.enter_context(nc.sbuf_tensor("s_crow", [1, NP], F32))
            p_q2 = [ph.enter_context(nc.psum_tensor(f"p_q{i}", [128, CH], F32)) for i in range(2)]
            p_nq2 = [ph.enter_context(nc.psum_tensor(f"p_nq{i}", [1, CH], F32)) for i in range(2)]
            p_nc = ph.enter_context(nc.psum_tensor("p_nc", [128, NP], F32))
            nstq = {"i": 0, "scg": [0, 0], "veg": [0, 0]}

            wtq_snap = DMA("sync", s_wtqq[:], wtq_sp.rearrange("i p n -> p i n"), "IN")
            x_snap[0] = DMA("sync", s_xnq2[0][:],
                            xh_sp.rearrange("i p t -> p i t")[:, :, 0:CH], "IN")
            pkg = [0, 0]
            qsg = [0, 0]
            qfree = [0] * NCH

            for c in range(NCH):
                buf = c % 2
                qb = c % 2
                t0 = c * CH
                if c + 1 < NCH:
                    if c >= 1:
                        W("sync", "TN", qfree[c - 1])
                    x_snap[c + 1] = DMA("sync", s_xnq2[1 - buf][:],
                                        xh_sp.rearrange("i p t -> p i t")[:, :, t0 + CH:t0 + 2 * CH], "IN")
                W("tn", "IN", x_snap[c])
                for pr in range(NP):
                    b = pr % 2
                    W("tn", "VE", pkg[b])
                    for fi in range(NF):
                        q_done = O("tn", (lambda e, fi=fi, pr=pr, b=b, buf=buf: e.matmul(
                            p_q2[b][:, :], s_wtqq[:, fi, pr * 128:(pr + 1) * 128],
                            s_xnq2[buf][:, fi, :], start=(fi == 0), stop=(fi == NF - 1))), "TN")
                    W("ve", "TN", q_done)
                    if pr < 2:
                        W("ve", "SP", qsg[qb])
                    pkg[b] = O("ve", (lambda e, b=b, qb=qb, pr=pr: e.tensor_copy(
                        s_qtile[qb][:, pr, :], p_q2[b][:, :])), "VE")
                    W("sc", "VE", pkg[b])
                    emit_norm(s_qtile[qb][:, pr, :], 1, pr, s_sqq2, p_nq2, nstq)
                qfree[c] = q_done
                W("gp", "VE", pkg[(NP - 1) % 2])
                W("gp", "VE", pkg[NP % 2])
                qsg[qb] = DMA("gp", qt_sp[:, :, t0:t0 + CH].rearrange("r p t -> p r t"),
                              s_qtile[qb][:], "SP")

            c2_done = O("ve", (lambda e: e.tensor_mul(s_crow[:, :], s_nmax[:, 0, :], s_nmax[:, 1, :])), "VE")
            W("sc", "VE", c2_done)
            ln_done = O("sc", (lambda e: e.activation(s_crow[:, :], s_crow[:, :], AF.Ln)), "SC")
            W("ve", "SC", ln_done)
            aff_done = O("ve", (lambda e: e.tensor_scalar(s_crow[:, :], s_crow[:, :],
                                                          0.5, 0.01, ALU.mult, ALU.add)), "VE")
            W("sc", "VE", aff_done)
            cc_done = O("sc", (lambda e: e.activation(s_crow[:, :], s_crow[:, :], AF.Exp)), "SC")
            W("ve", "SC", cc_done)
            O("ve", (lambda e: e.tensor_scalar_min(s_crow[:, :], s_crow[:, :], 60.0)), "VE")
            cn_done = O("ve", (lambda e: e.tensor_scalar_mul(s_crow[:, :], s_crow[:, :], -1.0)), "VE")
            W("tn", "VE", cn_done)
            ncb = O("tn", (lambda e: e.matmul(p_nc[:, :], s_onesr[:, :], s_crow[:, :],
                                              start=True, stop=True, skip_group_check=True)), "TN")
            W("ve", "TN", ncb)
            O("ve", (lambda e: e.tensor_copy(s_negc[:], p_nc[:, :])), "VE")
            S.barrier()

        with ExitStack() as ph:
            s_mhb = ph.enter_context(nc.sbuf_tensor("s_mhb", [128, NF, D], BF16))
            s_mlb = ph.enter_context(nc.sbuf_tensor("s_mlb", [128, NF, D], BF16))
            s_bias = ph.enter_context(nc.sbuf_tensor("s_bias", [128, NF], F32))
            s_xhb = [ph.enter_context(nc.sbuf_tensor(f"s_xhb{i}", [128, NF, CH], BF16)) for i in range(2)]
            s_xlb = [ph.enter_context(nc.sbuf_tensor(f"s_xlb{i}", [128, NF, CH], BF16)) for i in range(2)]
            s_part = [ph.enter_context(nc.sbuf_tensor(f"s_part{i}", [128, CH], F32)) for i in range(2)]
            p_f = [ph.enter_context(nc.psum_tensor(f"p_f{i}", [128, CH], F32)) for i in range(2)]

            DMA("sync", s_mhb[:], mh_sp.rearrange("i p n -> p i n"), "IN")
            m_loaded = DMA("sync", s_mlb[:], ml_sp.rearrange("i p n -> p i n"), "IN")
            DMA("sync", s_bias[:], biasv[:, :], "IN")
            DMA("sync", s_x[0][:], x_fm.rearrange("i p t -> p i t")[:, :, 0:CH], "IN")
            DMA("sync", s_xhb[0][:], xh_sp.rearrange("i p t -> p i t")[:, :, 0:CH], "IN")
            x_snap[0] = DMA("sync", s_xlb[0][:], xl_sp.rearrange("i p t -> p i t")[:, :, 0:CH], "IN")
            pf_guard = [0, 0]
            spg = [0, 0]
            bfree = [0] * NCH

            for c in range(NCH):
                buf = c % 2
                t0 = c * CH
                if c + 1 < NCH:
                    if c >= 1:
                        W("sync", "TN", bfree[c - 1])
                        W("sync", "VE", xfree_snap[c - 1])
                    DMA("sync", s_x[1 - buf][:],
                        x_fm.rearrange("i p t -> p i t")[:, :, t0 + CH:t0 + 2 * CH], "IN")
                    DMA("sync", s_xhb[1 - buf][:],
                        xh_sp.rearrange("i p t -> p i t")[:, :, t0 + CH:t0 + 2 * CH], "IN")
                    x_snap[c + 1] = DMA("sync", s_xlb[1 - buf][:],
                                        xl_sp.rearrange("i p t -> p i t")[:, :, t0 + CH:t0 + 2 * CH], "IN")
                W("tn", "IN", x_snap[c])
                if c == 0:
                    W("tn", "IN", m_loaded)
                last_pv = 0
                for fo in range(NF):
                    b = fo % 2
                    W("tn", "SC", pf_guard[b])
                    for ti, (xt, mt) in enumerate(((s_xhb[buf], s_mhb), (s_xlb[buf], s_mhb),
                                                   (s_xhb[buf], s_mlb))):
                        for fi in range(NF):
                            f_done = O("tn", (lambda e, fi=fi, fo=fo, b=b, xt=xt, mt=mt,
                                              st=(ti == 0 and fi == 0),
                                              sp=(ti == 2 and fi == NF - 1): e.matmul(
                                p_f[b][:, :], mt[:, fi, fo * 128:(fo + 1) * 128],
                                xt[:, fi, :], start=st, stop=sp)), "TN")
                    W("sc", "TN", f_done)
                    W("sc", "SP", spg[b])
                    pf_guard[b] = O("sc", (lambda e, fo=fo, b=b: e.add(
                        s_part[b][:, :], p_f[b][:, :], s_bias[:, fo:fo + 1])), "SC")
                    W("ve", "SC", pf_guard[b])
                    last_pv = O("ve", (lambda e, fo=fo, b=b, buf=buf: e.tensor_add(
                        s_part[b][:, :], s_part[b][:, :], s_x[buf][:, fo, :])), "VE")
                    W("gp", "VE", last_pv)
                    spg[b] = DMA("gp", part_sp[fo, :, t0:t0 + CH], s_part[b][:], "SP")
                bfree[c] = f_done
                xfree_snap[c] = last_pv
            S.barrier()

    with ExitStack() as ph:
        s_wo = ph.enter_context(nc.sbuf_tensor("s_wo", [128, NF, D], BF16))
        s_qt = [ph.enter_context(nc.sbuf_tensor(f"s_qt{i}", [128, NP, CH], BF16)) for i in range(2)]
        s_pt = [ph.enter_context(nc.sbuf_tensor(f"s_pt{i}", [128, NF, CH], F32)) for i in range(2)]
        s_ring = ph.enter_context(nc.sbuf_tensor("s_ring", [128, 3, 2, KPC, CH], BF16))
        s_mask = ph.enter_context(nc.sbuf_tensor("s_mask", [128, KPC, CH], BF16))
        s_att = ph.enter_context(nc.sbuf_tensor("s_att", [65, H, CH], BF16))
        s_selc = ph.enter_context(nc.sbuf_tensor("s_selc", [65, H, H], BF16))
        s_selb = ph.enter_context(nc.sbuf_tensor("s_selb", [H, NP, 128], F32))
        s_den = ph.enter_context(nc.sbuf_tensor("s_den", [H, CH], F32))
        s_rec = ph.enter_context(nc.sbuf_tensor("s_rec", [H, CH], F32))
        s_afm = ph.enter_context(nc.sbuf_tensor("s_afm", [128, NF, CH], BF16))
        s_out = [ph.enter_context(nc.sbuf_tensor(f"s_out{i}", [128, CH], F32)) for i in range(2)]

        p_sq = [ph.enter_context(nc.psum_tensor(f"p_sq{i}", [128, 2, KPC, CH], F32)) for i in range(3)]
        p_att = ph.enter_context(nc.psum_tensor("p_att", [65, 2, CH], F32))
        p_misc = ph.enter_context(nc.psum_tensor("p_misc", [128, 2, CH], F32))
        p_den = p_misc[0:16, 0, :]
        p_bc = p_misc[:, 1, :]

        wo_snap = DMA("gp", s_wo[:], w_o.rearrange("(i p) n -> p i n", p=128), "SP")
        DMA("sync", s_mask[:], masks.rearrange("j p t -> p j t"), "IN")
        DMA("sync", s_selc[:], selcol[:, :, :], "IN")
        DMA("sync", s_selb[:], selbc[:, :, :], "IN")
        tile_snap = [0] * NCH
        DMA("sync", s_qt[0][:], qt_sp.rearrange("r p t -> p r t")[:, :, 0:CH], "IN")
        tile_snap[0] = DMA("sync", s_pt[0][:], part_sp.rearrange("i p t -> p i t")[:, :, 0:CH], "IN")

        tilefree = [0] * NCH
        expdone = [0, 0, 0]
        ringfree = [0, 0, 0]
        att_evac = 0
        outfree = [0, 0]
        po_guard = [0, 0]
        den_copy = 0
        bc_last = 0

        for c in range(NCH):
            buf = c % 2
            t0 = c * CH
            nkt = KPC * (c + 1)
            if c + 1 < NCH:
                if c >= 1:
                    W("sync", "VE", tilefree[c - 1])
                DMA("sync", s_qt[1 - buf][:],
                    qt_sp.rearrange("r p t -> p r t")[:, :, t0 + CH:t0 + 2 * CH], "IN")
                tile_snap[c + 1] = DMA("sync", s_pt[1 - buf][:],
                                       part_sp.rearrange("i p t -> p i t")[:, :, t0 + CH:t0 + 2 * CH], "IN")
            qt = s_qt[buf]
            W("tn", "IN", tile_snap[c])

            def emit_attnv(pgi, ptiles, psnap, pmask, pr, last):
                nonlocal att_evac
                W("tn", "SC", psnap)
                if pmask is not None:
                    W("tn", "VE", pmask)
                if pgi == 0:
                    W("tn", "VE", att_evac)
                gpar = pgi % 3
                for ji, kt in enumerate(ptiles):
                    first = (pgi == 0 and ji == 0)
                    O("tn", (lambda e, kt=kt, ji=ji, pr=pr, first=first, gpar=gpar: e.matmul(
                        p_att[:, 0, :], v_s[:, kt, 2 * pr, :], s_ring[:, gpar, 0, ji, :],
                        start=first, stop=False, skip_group_check=True)), None)
                    done = O("tn", (lambda e, kt=kt, ji=ji, pr=pr, gpar=gpar,
                                    sp=(last and ji == len(ptiles) - 1): e.matmul(
                        p_att[:, 1, :], v_s[:, kt, 2 * pr + 1, :], s_ring[:, gpar, 1, ji, :],
                        start=False, stop=sp, skip_group_check=True)), "TN")
                ringfree[gpar] = done
                return done

            for pr in range(NP):
                groups = [list(range(g, g + KPC)) for g in range(0, nkt, KPC)]
                prev = None
                for gi, tiles in enumerate(groups):
                    gpar = gi % 3
                    W("tn", "SC", expdone[gpar])
                    if gpar == 0:
                        W("tn", "VE", po_guard[0])
                        W("tn", "VE", po_guard[1])
                    for ji, kt in enumerate(tiles):
                        O("tn", (lambda e, kt=kt, ji=ji, pr=pr, gpar=gpar, qt=qt: e.matmul(
                            p_sq[gpar][:, 0, ji, :], k_s[0:64, pr, kt * KT:(kt + 1) * KT],
                            qt[0:64, pr, :], start=(ji == 0), stop=(ji == KPC - 1),
                            skip_group_check=True)), None)
                        sc_snap = O("tn", (lambda e, kt=kt, ji=ji, pr=pr, gpar=gpar, qt=qt: e.matmul(
                            p_sq[gpar][:, 1, ji, :], k_s[64:128, pr, kt * KT:(kt + 1) * KT],
                            qt[64:128, pr, :], start=(ji == 0), stop=(ji == KPC - 1),
                            skip_group_check=True)), "TN")
                    W("sc", "TN", sc_snap)
                    W("sc", "TN", ringfree[gpar])
                    expdone[gpar] = O("sc", (lambda e, gpar=gpar, pr=pr: e.activation(
                        s_ring[:, gpar, :, :, :], p_sq[gpar][:, :, :, :], AF.Exp,
                        bias=s_negc[:, pr:pr + 1])), "SC")
                    mask_done = None
                    for ji, kt in enumerate(tiles):
                        j = kt - KPC * c
                        if j >= 0:
                            W("ve", "SC", expdone[gpar])
                            O("ve", (lambda e, gpar=gpar, ji=ji, j=j: e.tensor_mul(
                                s_ring[:, gpar, 0, ji, :], s_ring[:, gpar, 0, ji, :],
                                s_mask[:, j, :])), "VE")
                            mask_done = O("ve", (lambda e, gpar=gpar, ji=ji, j=j: e.tensor_mul(
                                s_ring[:, gpar, 1, ji, :], s_ring[:, gpar, 1, ji, :],
                                s_mask[:, j, :])), "VE")
                    if prev is not None:
                        emit_attnv(*prev, pr, False)
                    prev = (gi, tiles, expdone[gpar], mask_done)
                attnv_done = emit_attnv(*prev, pr, True)
                W("ve", "TN", attnv_done)
                O("ve", (lambda e, pr=pr: e.tensor_copy(s_att[:, 2 * pr, :], p_att[:, 0, :])), "VE")
                att_evac = O("ve", (lambda e, pr=pr: e.tensor_copy(
                    s_att[:, 2 * pr + 1, :], p_att[:, 1, :])), "VE")
                W("tn", "VE", att_evac)
                if pr == 0:
                    W("tn", "VE", bc_last)
                for hh in (2 * pr, 2 * pr + 1):
                    den_done = O("tn", (lambda e, hh=hh: e.matmul(
                        p_den, s_selc[:, hh, :], s_att[:, hh, :],
                        start=(hh == 0), stop=(hh == H - 1), skip_group_check=True)), "TN")

            W("ve", "TN", den_done)
            den_copy = O("ve", (lambda e: e.tensor_copy(s_den[0:16, :], p_den)), "VE")
            rec_done = O("ve", (lambda e: e.reciprocal(s_rec[0:16, :], s_den[0:16, :])), "VE")
            W("tn", "VE", den_copy)
            bc_guard = rec_done
            for pr in range(NP):
                W("tn", "VE", bc_guard)
                bcm = O("tn", (lambda e, pr=pr: e.matmul(
                    p_bc, s_selb[:, pr, :], s_rec[0:16, :],
                    start=True, stop=True, skip_group_check=True)), "TN")
                W("ve", "TN", bcm)
                O("ve", (lambda e, pr=pr: e.tensor_mul(
                    s_afm[0:64, pr, :], s_att[0:64, 2 * pr, :], p_misc[0:64, 1, :])), "VE")
                bc_guard = O("ve", (lambda e, pr=pr: e.tensor_mul(
                    s_afm[64:128, pr, :], s_att[0:64, 2 * pr + 1, :], p_misc[64:128, 1, :])), "VE")
            bc_last = bc_guard
            if c == 0:
                W("tn", "SP", wo_snap)
            for fo in range(NF):
                b = fo % 2
                if fo == 0:
                    W("tn", "VE", bc_guard)
                    W("tn", "SC", expdone[0])
                W("tn", "VE", po_guard[b])
                for fi in range(NF):
                    wo_done = O("tn", (lambda e, fi=fi, fo=fo, b=b: e.matmul(
                        p_sq[0][:, b, 0, :], s_wo[:, fi, fo * 128:(fo + 1) * 128],
                        s_afm[:, fi, :], start=(fi == 0), stop=(fi == NF - 1),
                        skip_group_check=True)), "TN")
                W("ve", "TN", wo_done)
                W("ve", "IN", tile_snap[c])
                W("ve", "SP", outfree[b])
                og = O("ve", (lambda e, fo=fo, b=b, buf=buf: e.scalar_tensor_tensor(
                    s_out[b][:, :], p_sq[0][:, b, 0, :], 1.0, s_pt[buf][:, fo, :],
                    ALU.mult, ALU.add)), "VE")
                po_guard[b] = og
                W("gp", "VE", og)
                outfree[b] = DMA("gp", out_fm[fo, :, t0:t0 + CH], s_out[b][:], "SP")
            tilefree[c] = po_guard[(NF - 1) % 2]

        S.wait("gp", "SP")

    def replay(eng):
        def run(e):
            for item in S.ops[eng]:
                if item[0] == "w":
                    e.wait_ge(sems[item[1]], item[2])
                else:
                    instr = item[1](e)
                    if item[2] is not None:
                        instr.then_inc(sems[item[2]], item[3])
        return run

    with nc.Block() as block:
        block.sync(replay("sync"))
        block.gpsimd(replay("gp"))
        block.vector(replay("ve"))
        block.scalar(replay("sc"))
        block.tensor(replay("tn"))

    top.close()
    return nc


_NC_CACHE = None


def _get_program():
    global _NC_CACHE
    if _NC_CACHE is None:
        _NC_CACHE = build_program()
    return _NC_CACHE


def kernel(L0_attn_x, L0_attn_W_qkv, L0_attn_W_q_lsr, L0_attn_W_k_lsr, L0_attn_core,
           L0_attn_W_o, L0_ffn_w_freq_real, L0_ffn_w_freq_imag, L0_ffn_bias):
    x = np.asarray(L0_attn_x, np.float32)
    W_qkv = np.ascontiguousarray(np.asarray(L0_attn_W_qkv, np.float32))
    W_qlsr = np.asarray(L0_attn_W_q_lsr, np.float32)
    W_klsr = np.asarray(L0_attn_W_k_lsr, np.float32)
    core = np.asarray(L0_attn_core, np.float32)
    W_o_np = np.ascontiguousarray(np.asarray(L0_attn_W_o, np.float32))
    Wr = np.ascontiguousarray(np.asarray(L0_ffn_w_freq_real, np.float32))
    Wi = np.ascontiguousarray(np.asarray(L0_ffn_w_freq_imag, np.float32))
    biasn = np.asarray(L0_ffn_bias, np.float32)

    jk = np.outer(np.arange(D, dtype=np.float64), np.arange(D, dtype=np.float64))
    ang = 2.0 * np.pi * (jk % D) / D
    C_np = np.ascontiguousarray(np.cos(ang).astype(np.float32))
    S_np = np.ascontiguousarray(np.sin(ang).astype(np.float32))
    masks_np = np.zeros((KPC, 128, CH), BFNP)
    for j in range(KPC):
        for kk in range(128):
            masks_np[j, kk, :] = (np.arange(CH) >= (128 * j + kk)).astype(BFNP)
    selcol_np = np.zeros((65, H, H), BFNP)
    for h in range(H):
        selcol_np[64, h, h] = 1.0
    selbc_np = np.zeros((H, NP, 128), np.float32)
    for pr in range(NP):
        selbc_np[2 * pr, pr, 0:64] = 1.0
        selbc_np[2 * pr + 1, pr, 64:128] = 1.0

    shared = {
        "w_kv": np.ascontiguousarray(W_qkv[:, D:3 * D]),
        "wqT": np.ascontiguousarray(
            W_qkv[:, 0:D].T.reshape(NP, 2, DH, D).transpose(1, 2, 0, 3).reshape(128, NP, D)),
        "qlsrT": np.ascontiguousarray(W_qlsr.transpose(2, 0, 1)),
        "klsrT": np.ascontiguousarray(W_klsr.transpose(2, 0, 1)),
        "coreT": np.ascontiguousarray(core.T),
        "w_o": W_o_np, "w_fr": Wr, "w_fi": Wi,
        "biasv": np.ascontiguousarray(biasn.reshape(NF, 128).T),
        "cmat": C_np, "smat": S_np, "masks": masks_np, "selcol": selcol_np,
        "selbc": selbc_np,
        "onescol": np.full((128, 1), 1.0 / D, np.float32),
        "ones128": np.ones((128, 1), np.float32),
        "onesrow": np.ones((1, 128), np.float32),
        "cneg": np.ascontiguousarray(-C_np),
        "onesbf": np.full((128, 1), 1.0 / D, BFNP),
        "ones1b": np.ones((128, 1), BFNP),
    }
    in_maps = []
    for c in range(NC):
        m = dict(shared)
        m["x_fm"] = np.ascontiguousarray(x[c].T).reshape(NF, 128, T)
        in_maps.append(m)

    res = run_bass_kernel_spmd(_get_program(), in_maps, core_ids=list(range(NC)))
    outs = []
    for c in range(NC):
        o = np.asarray(res.results[c]["out_fm"]).reshape(D, T)
        outs.append(np.ascontiguousarray(o.T))
    return np.stack(outs).astype(np.float32)


# version 25 (final)
# speedup vs baseline: 1.1602x; 1.0433x over previous
import numpy as np
import ml_dtypes
import concourse.bass as bass
import concourse.mybir as mybir
from concourse.bass_utils import run_bass_kernel_spmd
from contextlib import ExitStack

F32 = mybir.dt.float32
F32R = mybir.dt.float32r
BF16 = mybir.dt.bfloat16
AF = mybir.ActivationFunctionType
ALU = mybir.AluOpType
AX = mybir.AxisListType
BFNP = ml_dtypes.bfloat16

NC = 8
D = 1024
T = 4096
H = 16
DH = 64
NP = 8
NF = 8
CH = 256
NCH = T // CH
KT = 128
KPC = CH // KT
NKT = T // KT
EPS = 1e-5

ENGS = ("sync", "gp", "ve", "sc", "tn")
SEM_NAMES = ("IN", "SP", "TN", "VE", "SC")


class Sched:
    def __init__(self):
        self.ops = {e: [] for e in ENGS}
        self.cnt = {s: 0 for s in SEM_NAMES}

    def wait(self, eng, sem, val=None):
        v = self.cnt[sem] if val is None else val
        if v > 0:
            self.ops[eng].append(("w", sem, v))

    def op(self, eng, fn, sem=None, amt=1):
        if sem is not None:
            self.cnt[sem] += amt
            self.ops[eng].append(("i", fn, sem, amt))
            return self.cnt[sem]
        self.ops[eng].append(("i", fn, None, 0))
        return None

    def dma(self, eng, out, in_, sem):
        return self.op(eng, (lambda e, o=out, i=in_: e.dma_start(out=o, in_=i)), sem, 16)

    def barrier(self):
        snap = dict(self.cnt)
        for e in ENGS:
            for s in SEM_NAMES:
                self.wait(e, s, snap[s])


def build_program():
    nc = bass.Bass("TRN2", target_bir_lowering=False, debug=False)

    x_fm = nc.dram_tensor("x_fm", [NF, 128, T], F32, kind="ExternalInput").ap()
    w_kv = nc.dram_tensor("w_kv", [D, 2 * D], F32, kind="ExternalInput").ap()
    wqT = nc.dram_tensor("wqT", [128, NP, D], F32, kind="ExternalInput").ap()
    qlsrT = nc.dram_tensor("qlsrT", [16, H, DH], F32, kind="ExternalInput").ap()
    klsrT = nc.dram_tensor("klsrT", [16, H, DH], F32, kind="ExternalInput").ap()
    coreT = nc.dram_tensor("coreT", [16, H], F32, kind="ExternalInput").ap()
    w_o = nc.dram_tensor("w_o", [D, D], F32, kind="ExternalInput").ap()
    w_fr = nc.dram_tensor("w_fr", [D, D], F32, kind="ExternalInput").ap()
    w_fi = nc.dram_tensor("w_fi", [D, D], F32, kind="ExternalInput").ap()
    biasv = nc.dram_tensor("biasv", [128, NF], F32, kind="ExternalInput").ap()
    cmat = nc.dram_tensor("cmat", [D, D], F32, kind="ExternalInput").ap()
    smat = nc.dram_tensor("smat", [D, D], F32, kind="ExternalInput").ap()
    masks = nc.dram_tensor("masks", [KPC, 128, CH], BF16, kind="ExternalInput").ap()
    selcol = nc.dram_tensor("selcol", [65, H, H], BF16, kind="ExternalInput").ap()
    selbc = nc.dram_tensor("selbc", [H, NP, 128], F32, kind="ExternalInput").ap()
    onescol = nc.dram_tensor("onescol", [128, 1], F32, kind="ExternalInput").ap()
    ones128 = nc.dram_tensor("ones128", [128, 1], F32, kind="ExternalInput").ap()
    onesrow = nc.dram_tensor("onesrow", [1, 128], F32, kind="ExternalInput").ap()
    cneg = nc.dram_tensor("cneg", [D, D], F32, kind="ExternalInput").ap()
    onesbf = nc.dram_tensor("onesbf", [128, 1], BF16, kind="ExternalInput").ap()
    ones1b = nc.dram_tensor("ones1b", [128, 1], BF16, kind="ExternalInput").ap()

    out_fm = nc.dram_tensor("out_fm", [NF, 128, T], F32, kind="ExternalOutput").ap()

    wtq_sp = nc.dram_tensor("wtq_sp", [NF, 128, D], BF16, kind="Internal").ap()
    at_sp = nc.dram_tensor("at_sp", [NF, 128, D], F32, kind="Internal").ap()
    nbt_sp = nc.dram_tensor("nbt_sp", [NF, 128, D], F32, kind="Internal").ap()
    mh_sp = nc.dram_tensor("mh_sp", [NF, 128, D], BF16, kind="Internal").ap()
    ml_sp = nc.dram_tensor("ml_sp", [NF, 128, D], BF16, kind="Internal").ap()
    qt_sp = nc.dram_tensor("qt_sp", [NP, 128, T], BF16, kind="Internal").ap()
    xh_sp = nc.dram_tensor("xh_sp", [NF, 128, T], BF16, kind="Internal").ap()
    xl_sp = nc.dram_tensor("xl_sp", [NF, 128, T], BF16, kind="Internal").ap()
    part_sp = nc.dram_tensor("part_sp", [NF, 128, T], F32, kind="Internal").ap()

    top = ExitStack()
    sems = {s: top.enter_context(nc.semaphore(s)) for s in SEM_NAMES}
    S = Sched()
    W, O, DMA = S.wait, S.op, S.dma

    s_ones = top.enter_context(nc.sbuf_tensor("s_ones", [128, 1], F32))
    s_ones1 = top.enter_context(nc.sbuf_tensor("s_ones1", [128, 1], F32))
    s_onesr = top.enter_context(nc.sbuf_tensor("s_onesr", [1, 128], F32))
    s_onesb = top.enter_context(nc.sbuf_tensor("s_onesb", [128, 1], BF16))
    s_ones1b = top.enter_context(nc.sbuf_tensor("s_ones1b", [128, 1], BF16))

    DMA("sync", s_ones[:], onescol[:, :], "IN")
    DMA("sync", s_ones1[:], ones128[:, :], "IN")
    DMA("sync", s_onesr[:], onesrow[:, :], "IN")
    DMA("sync", s_onesb[:], onesbf[:, :], "IN")
    DMA("sync", s_ones1b[:], ones1b[:, :], "IN")

    ph0w = ExitStack()
    s_C = ph0w.enter_context(nc.sbuf_tensor("s_C", [128, NF, D], F32R))
    s_Sm = ph0w.enter_context(nc.sbuf_tensor("s_Sm", [128, NF, D], F32R))
    s_Wr = ph0w.enter_context(nc.sbuf_tensor("s_Wr", [128, NF, D], F32R))
    s_Wi = ph0w.enter_context(nc.sbuf_tensor("s_Wi", [128, NF, D], F32R))
    for src, dst in ((cmat, s_C), (smat, s_Sm), (w_fr, s_Wr), (w_fi, s_Wi)):
        DMA("gp", dst[:], src.rearrange("(i p) n -> p i n", p=128), "SP")

    with ExitStack() as ph:
        s_wqT = ph.enter_context(nc.sbuf_tensor("s_wqT", [128, NP, D], F32))
        s_qlsr = ph.enter_context(nc.sbuf_tensor("s_qlsr", [16, H, DH], F32))
        s_klsr = ph.enter_context(nc.sbuf_tensor("s_klsr", [16, H, DH], F32))
        s_coreT = ph.enter_context(nc.sbuf_tensor("s_coreT", [16, H], F32))
        s_qsc = ph.enter_context(nc.sbuf_tensor("s_qsc", [16, H, DH], F32))
        s_G = ph.enter_context(nc.sbuf_tensor("s_G", [128, NP, DH], F32))
        s_wtq = ph.enter_context(nc.sbuf_tensor("s_wtq0", [128, NF, D], BF16))
        p_G = ph.enter_context(nc.psum_tensor("p_G", [DH, H, DH], F32))
        p_wq = [ph.enter_context(nc.psum_tensor(f"p_wq{i}", [128, NF, DH], F32)) for i in range(2)]

        DMA("sync", s_wqT[:], wqT[:, :, :], "IN")
        DMA("sync", s_qlsr[:], qlsrT[:, :, :], "IN")
        DMA("sync", s_klsr[:], klsrT[:, :, :], "IN")
        DMA("sync", s_coreT[:], coreT[:, :], "IN")
        W("ve", "IN")
        for h in range(H):
            qsc_done = O("ve", (lambda e, h=h: e.tensor_scalar(
                s_qsc[:, h, :], s_qlsr[:, h, :], s_coreT[:, h:h + 1], 0.25,
                ALU.mult, ALU.mult)), "VE")
        W("tn", "VE", qsc_done)
        for h in range(H):
            g_done = O("tn", (lambda e, h=h: e.matmul(
                p_G[:, h, :], s_qsc[:, h, :], s_klsr[:, h, :],
                start=(h % 8 == 0), stop=True, skip_group_check=True)), "TN")
        W("ve", "TN", g_done)
        O("ve", (lambda e: e.tensor_copy(s_G[0:64, :, :], p_G[:, 0:H:2, :])), "VE")
        sg_done = O("ve", (lambda e: e.tensor_copy(s_G[64:128, :, :], p_G[:, 1:H:2, :])), "VE")
        W("tn", "VE", sg_done)
        wq_evac = [0, 0]
        for h in range(H):
            b = h % 2
            hp = h % 2
            pri = h // 2
            W("tn", "VE", wq_evac[b])
            for fi in range(NF):
                t_done = O("tn", (lambda e, hp=hp, pri=pri, fi=fi, b=b: e.matmul(
                    p_wq[b][:, fi, :],
                    s_wqT[hp * 64:(hp + 1) * 64, pri, fi * 128:(fi + 1) * 128],
                    s_G[hp * 64:(hp + 1) * 64, pri, :],
                    start=(fi == 0), stop=(fi == NF - 1), skip_group_check=True)), "TN")
            W("ve", "TN", t_done)
            wq_evac[b] = O("ve", (lambda e, h=h, b=b: e.tensor_copy(
                s_wtq[:, 0:NF, h * DH:(h + 1) * DH], p_wq[b][:, :, :])), "VE")
        W("gp", "VE")
        DMA("gp", wtq_sp.rearrange("i p n -> p i n"), s_wtq[:], "SP")
        S.barrier()

    with ExitStack() as ph:
        p_st = [ph.enter_context(nc.psum_tensor(f"p_st{i}", [128, 512], F32)) for i in range(3)]
        evsnap = [0, 0, 0]
        bank = [0]

        def stage_mm(terms, dst_sp, s_tmp, scale=None, mh=None):
            tsp_free = [0, 0]
            for oc in range(NF):
                for wi in range(2):
                    b = bank[0] % 3
                    bank[0] += 1
                    W("tn", "VE", evsnap[b])
                    nterm = len(terms)
                    for ti, (lt, rt) in enumerate(terms):
                        for dc in range(NF):
                            t_done = O("tn", (lambda e, lt=lt, rt=rt, dc=dc, oc=oc, wi=wi, b=b,
                                              st=(ti == 0 and dc == 0),
                                              sp=(ti == nterm - 1 and dc == NF - 1): e.matmul(
                                p_st[b][:, :],
                                lt[:, dc, oc * 128:(oc + 1) * 128],
                                rt[:, dc, wi * 512:(wi + 1) * 512],
                                start=st, stop=sp)), "TN")
                    W("ve", "TN", t_done)
                    tb = wi
                    W("ve", "SP", tsp_free[tb])
                    if scale is None:
                        ev = O("ve", (lambda e, b=b, tb=tb: e.tensor_copy(
                            s_tmp[tb][:, :], p_st[b][:, :])), "VE")
                    else:
                        ev = O("ve", (lambda e, b=b, tb=tb: e.tensor_scalar_mul(
                            s_tmp[tb][:, :], p_st[b][:, :], scale)), "VE")
                    if mh is not None:
                        mh_sp_, ml_sp_, s_mh, s_ml = mh
                        O("ve", (lambda e, tb=tb, s_mh=s_mh, s_tmp=s_tmp: e.tensor_copy(
                            s_mh[tb][:, :], s_tmp[tb][:, :])), "VE")
                        ev = O("ve", (lambda e, tb=tb, s_mh=s_mh, s_ml=s_ml, s_tmp=s_tmp: e.tensor_sub(
                            s_ml[tb][:, :], s_tmp[tb][:, :], s_mh[tb][:, :])), "VE")
                    evsnap[b] = ev
                    W("gp", "VE", ev)
                    if mh is None:
                        tsp_free[tb] = DMA("gp", dst_sp[oc, :, wi * 512:(wi + 1) * 512],
                                           s_tmp[tb][:], "SP")
                    else:
                        mh_sp_, ml_sp_, s_mh, s_ml = mh
                        DMA("gp", mh_sp_[oc, :, wi * 512:(wi + 1) * 512], s_mh[tb][:], "SP")
                        tsp_free[tb] = DMA("gp", ml_sp_[oc, :, wi * 512:(wi + 1) * 512],
                                           s_ml[tb][:], "SP")

        with ExitStack() as ph1:
            s_tmp = [ph1.enter_context(nc.sbuf_tensor(f"s_tmp{i}", [128, 512], F32)) for i in range(2)]
            W("tn", "SP")
            stage_mm([(s_Wr, s_C), (s_Wi, s_Sm)], at_sp, s_tmp)
            W("gp", "TN")
            cneg_snap = DMA("gp", s_C[:], cneg.rearrange("(i p) n -> p i n", p=128), "SP")
            W("tn", "SP", cneg_snap)
            stage_mm([(s_Wr, s_Sm), (s_Wi, s_C)], nbt_sp, s_tmp)
            S.barrier()
        with ExitStack() as ph2:
            s_AT = ph2.enter_context(nc.sbuf_tensor("s_AT", [128, NF, D], F32R))
            s_NBT = ph2.enter_context(nc.sbuf_tensor("s_NBT", [128, NF, D], F32R))
            s_tmp2 = [ph2.enter_context(nc.sbuf_tensor(f"s_tm2{i}", [128, 512], F32)) for i in range(2)]
            s_mh0 = [ph2.enter_context(nc.sbuf_tensor(f"s_mh{i}", [128, 512], BF16)) for i in range(2)]
            s_ml0 = [ph2.enter_context(nc.sbuf_tensor(f"s_ml{i}", [128, 512], BF16)) for i in range(2)]
            c2_snap = DMA("gp", s_C[:], cmat.rearrange("(i p) n -> p i n", p=128), "SP")
            DMA("gp", s_AT[:], at_sp.rearrange("i p n -> p i n"), "SP")
            at_snap = DMA("gp", s_NBT[:], nbt_sp.rearrange("i p n -> p i n"), "SP")
            W("tn", "SP", at_snap)
            stage_mm([(s_AT, s_C), (s_NBT, s_Sm)], None, s_tmp2, scale=1.0 / D,
                     mh=(mh_sp, ml_sp, s_mh0, s_ml0))
            S.barrier()

    ph0w.close()

    k_s = top.enter_context(nc.sbuf_tensor("k_s", [128, NP, T], BF16))
    v_s = top.enter_context(nc.sbuf_tensor("v_s", [128, NKT, H, 65], BF16))
    s_negc = top.enter_context(nc.sbuf_tensor("s_negc", [128, NP], F32))

    with ExitStack() as ph1:
        s_x = [ph1.enter_context(nc.sbuf_tensor(f"s_x{i}", [128, NF, CH], F32)) for i in range(2)]

        x_snap = [0] * NCH
        xfree_snap = [0] * NCH

        def emit_ln(c, buf, out_bf=None):
            xb = s_x[buf]
            W("ve", "IN", x_snap[c])
            xbf_done = O("ve", (lambda e: e.tensor_copy(s_xbf[:], xb[:])), "VE")
            W("tn", "VE", xbf_done)
            for fi in range(NF):
                mean_done = O("tn", (lambda e, fi=fi: e.matmul(
                    p_stat[:, 0, :], s_onesb[:, :], s_xbf[:, fi, :],
                    start=(fi == 0), stop=(fi == NF - 1), skip_group_check=True)), "TN")
            W("sc", "VE", xbf_done)
            W("sc", "TN", mean_done)
            sq_done = O("sc", (lambda e: e.activation(s_xbf[:], s_xbf[:], AF.Square)), "SC")
            W("tn", "SC", sq_done)
            for fi in range(NF):
                m2_done = O("tn", (lambda e, fi=fi: e.matmul(
                    p_stat[:, 1, :], s_onesb[:, :], s_xbf[:, fi, :],
                    start=False, stop=(fi == NF - 1), skip_group_check=True)), "TN")
            W("ve", "TN", m2_done)
            O("ve", (lambda e: e.tensor_copy(s_stat[:, 0:2, :], p_stat[:, :, :])), "VE")
            O("ve", (lambda e: e.tensor_mul(s_stat[:, 2, :], s_stat[:, 0, :], s_stat[:, 0, :])), "VE")
            O("ve", (lambda e: e.tensor_sub(s_stat[:, 2, :], s_stat[:, 1, :], s_stat[:, 2, :])), "VE")
            var_done = O("ve", (lambda e: e.tensor_scalar_add(s_stat[:, 2, :], s_stat[:, 2, :], EPS)), "VE")
            W("sc", "VE", var_done)
            O("sc", (lambda e: e.activation(s_stat[:, 2, :], s_stat[:, 2, :], AF.Ln)), "SC")
            rstd_done = O("sc", (lambda e: e.activation(s_stat[:, 3, :], s_stat[:, 2, :], AF.Exp, scale=-0.5)), "SC")
            W("tn", "SC", rstd_done)
            O("tn", (lambda e: e.matmul(p_bcst[:, 0, :], s_onesr[:, :],
                                        s_stat[:, 0, :], start=True, stop=True,
                                        skip_group_check=True)), "TN")
            bc_done = O("tn", (lambda e: e.matmul(p_bcst[:, 1, :], s_onesr[:, :],
                                                  s_stat[:, 3, :], start=False, stop=True,
                                                  skip_group_check=True)), "TN")
            W("ve", "TN", bc_done)
            for fi in range(NF):
                O("ve", (lambda e, fi=fi: e.tensor_sub(s_xsq[:, fi, :], xb[:, fi, :], p_bcst[:, 0, :])), "VE")
                if out_bf is None:
                    xn_done = O("ve", (lambda e, fi=fi: e.tensor_mul(
                        s_xsq[:, fi, :], s_xsq[:, fi, :], p_bcst[:, 1, :])), "VE")
                else:
                    xn_done = O("ve", (lambda e, fi=fi: e.tensor_mul(
                        out_bf[:, fi, :], s_xsq[:, fi, :], p_bcst[:, 1, :])), "VE")
            return xn_done

        s_nmax = ph1.enter_context(nc.sbuf_tensor("s_nmax", [1, 2, NP], F32))
        s_ntmp = ph1.enter_context(nc.sbuf_tensor("s_ntmp", [1, 1], F32))
        O("ve", (lambda e: e.memset(s_nmax[:], 0.0)), "VE")

        def emit_norm(src_ap, which, pr, s_sq2, p_n2, st):
            b = st["i"] % 2
            st["i"] += 1
            W("sc", "TN", st["scg"][b])
            sq_done = O("sc", (lambda e, b=b, src_ap=src_ap, s_sq2=s_sq2: e.activation(
                s_sq2[:, b, :], src_ap, AF.Square)), "SC")
            W("tn", "SC", sq_done)
            W("tn", "VE", st["veg"][b])
            mm = O("tn", (lambda e, b=b, p_n2=p_n2, s_sq2=s_sq2: e.matmul(
                p_n2[b][:, :], s_ones1b[:, :], s_sq2[:, b, :],
                start=True, stop=True, skip_group_check=True)), "TN")
            st["scg"][b] = mm
            W("ve", "TN", mm)
            O("ve", (lambda e, b=b, p_n2=p_n2: e.tensor_reduce(
                s_ntmp[:, :], p_n2[b][:, :], AX.X, ALU.max)), "VE")
            st["veg"][b] = O("ve", (lambda e, pr=pr, which=which: e.tensor_max(
                s_nmax[:, which, pr:pr + 1], s_nmax[:, which, pr:pr + 1], s_ntmp[:, :])), "VE")

        with ExitStack() as ph:
            s_wkv = ph.enter_context(nc.sbuf_tensor("s_wkv", [128, NF, 2 * D], BF16))
            s_xsq = ph.enter_context(nc.sbuf_tensor("s_xsq", [128, NF, CH], F32))
            s_xbf = ph.enter_context(nc.sbuf_tensor("s_xbf", [128, NF, CH], BF16))
            s_stat = ph.enter_context(nc.sbuf_tensor("s_stat", [1, 4, CH], F32))
            p_stat = ph.enter_context(nc.psum_tensor("p_stat", [1, 2, CH], F32))
            p_bcst = ph.enter_context(nc.psum_tensor("p_bcst", [128, 2, CH], F32))
            s_xh = [ph.enter_context(nc.sbuf_tensor(f"s_xh{i}", [128, NF, CH], BF16)) for i in range(2)]
            s_xl = ph.enter_context(nc.sbuf_tensor("s_xl", [128, NF, CH], BF16))
            s_sq = ph.enter_context(nc.sbuf_tensor("s_sq", [128, 2, CH], BF16))
            p_k = [ph.enter_context(nc.psum_tensor(f"p_k{i}", [128, CH], F32)) for i in range(2)]
            p_v = [ph.enter_context(nc.psum_tensor(f"p_v{i}", [128, 512], F32)) for i in range(2)]
            p_n = [ph.enter_context(nc.psum_tensor(f"p_n{i}", [1, CH], F32)) for i in range(2)]
            nst = {"i": 0, "scg": [0, 0], "veg": [0, 0]}

            wkv_snap = DMA("gp", s_wkv[:], w_kv.rearrange("(i p) n -> p i n", p=128), "SP")
            vs_init = O("ve", (lambda e: e.memset(v_s[:], 1.0)), "VE")
            x_snap[0] = DMA("sync", s_x[0][:], x_fm.rearrange("i p t -> p i t")[:, :, 0:CH], "IN")

            pkg = [0, 0]
            pvg = [0, 0]
            xh_spilled = [0, 0]
            xl_spilled = 0

            for c in range(NCH):
                buf = c % 2
                t0 = c * CH
                if c + 1 < NCH:
                    if c >= 1:
                        W("sync", "VE", xfree_snap[c - 1])
                    x_snap[c + 1] = DMA("sync", s_x[1 - buf][:],
                                        x_fm.rearrange("i p t -> p i t")[:, :, t0 + CH:t0 + 2 * CH], "IN")
                xb2 = c % 2
                xn_done = emit_ln(c, buf, out_bf=None)
                xfree_snap[c] = xn_done
                if c >= 1:
                    W("ve", "SP", xh_spilled[1 - xb2])
                if c >= 1:
                    W("ve", "SP", xl_spilled)
                for fi in range(NF):
                    O("ve", (lambda e, fi=fi, xb2=xb2: e.tensor_copy(
                        s_xh[xb2][:, fi, :], s_xsq[:, fi, :])), "VE")
                    xh_done = O("ve", (lambda e, fi=fi, xb2=xb2: e.tensor_sub(
                        s_xl[:, fi, :], s_xsq[:, fi, :], s_xh[xb2][:, fi, :])), "VE")
                W("gp", "VE", xh_done)
                xh_spilled[xb2] = DMA("gp", xh_sp[:, :, t0:t0 + CH].rearrange("i p t -> p i t"),
                                      s_xh[xb2][:], "SP")
                xl_spilled = DMA("gp", xl_sp[:, :, t0:t0 + CH].rearrange("i p t -> p i t"),
                                 s_xl[:], "SP")
                W("tn", "VE", xh_done)
                if c == 0:
                    W("tn", "SP", wkv_snap)
                    W("tn", "VE", vs_init)
                for pr in range(NP):
                    b = pr % 2
                    W("tn", "VE", pkg[b])
                    for fi in range(NF):
                        k_done = O("tn", (lambda e, fi=fi, pr=pr, b=b, xb2=xb2: e.matmul(
                            p_k[b][:, :], s_wkv[:, fi, pr * 128:(pr + 1) * 128],
                            s_xh[xb2][:, fi, :], start=(fi == 0), stop=(fi == NF - 1))), "TN")
                    W("ve", "TN", k_done)
                    pkg[b] = O("ve", (lambda e, pr=pr, b=b, t0=t0: e.tensor_copy(
                        k_s[:, pr, t0:t0 + CH], p_k[b][:, :])), "VE")
                    W("sc", "VE", pkg[b])
                    emit_norm(k_s[:, pr, t0:t0 + CH], 0, pr, s_sq, p_n, nst)
                for j in range(KPC):
                    kt = c * KPC + j
                    for half in range(2):
                        W("tn", "VE", pvg[half])
                        for fi in range(NF):
                            v_done = O("tn", (lambda e, fi=fi, j=j, half=half, xb2=xb2: e.matmul(
                                p_v[half][:, :], s_xh[xb2][:, fi, j * KT:(j + 1) * KT],
                                s_wkv[:, fi, D + half * 512:D + (half + 1) * 512],
                                start=(fi == 0), stop=(fi == NF - 1))), "TN")
                        W("ve", "TN", v_done)
                        pvg[half] = O("ve", (lambda e, kt=kt, half=half: e.tensor_copy(
                            v_s[:, kt, half * 8:(half + 1) * 8, 0:DH],
                            p_v[half][:, :].rearrange("p (h d) -> p h d", d=DH))), "VE")
            S.barrier()

        with ExitStack() as ph:
            s_wtqq = ph.enter_context(nc.sbuf_tensor("s_wtqq1", [128, NF, D], BF16))
            s_xnq2 = [ph.enter_context(nc.sbuf_tensor(f"s_xnq2q{i}", [128, NF, CH], BF16)) for i in range(2)]
            s_qtile = [ph.enter_context(nc.sbuf_tensor(f"s_qtile{i}", [128, NP, CH], BF16)) for i in range(2)]
            s_sqq2 = ph.enter_context(nc.sbuf_tensor("s_sqq2q", [128, 2, CH], BF16))
            s_crow = ph.enter_context(nc.sbuf_tensor("s_crow", [1, NP], F32))
            p_q2 = [ph.enter_context(nc.psum_tensor(f"p_q{i}", [128, CH], F32)) for i in range(2)]
            p_nq2 = [ph.enter_context(nc.psum_tensor(f"p_nq{i}", [1, CH], F32)) for i in range(2)]
            p_nc = ph.enter_context(nc.psum_tensor("p_nc", [128, NP], F32))
            nstq = {"i": 0, "scg": [0, 0], "veg": [0, 0]}

            wtq_snap = DMA("sync", s_wtqq[:], wtq_sp.rearrange("i p n -> p i n"), "IN")
            x_snap[0] = DMA("sync", s_xnq2[0][:],
                            xh_sp.rearrange("i p t -> p i t")[:, :, 0:CH], "IN")
            pkg = [0, 0]
            qsg = [0, 0]
            qfree = [0] * NCH

            for c in range(NCH):
                buf = c % 2
                qb = c % 2
                t0 = c * CH
                if c + 1 < NCH:
                    if c >= 1:
                        W("sync", "TN", qfree[c - 1])
                    x_snap[c + 1] = DMA("sync", s_xnq2[1 - buf][:],
                                        xh_sp.rearrange("i p t -> p i t")[:, :, t0 + CH:t0 + 2 * CH], "IN")
                W("tn", "IN", x_snap[c])
                for pr in range(NP):
                    b = pr % 2
                    W("tn", "VE", pkg[b])
                    for fi in range(NF):
                        q_done = O("tn", (lambda e, fi=fi, pr=pr, b=b, buf=buf: e.matmul(
                            p_q2[b][:, :], s_wtqq[:, fi, pr * 128:(pr + 1) * 128],
                            s_xnq2[buf][:, fi, :], start=(fi == 0), stop=(fi == NF - 1))), "TN")
                    W("ve", "TN", q_done)
                    if pr < 2:
                        W("ve", "SP", qsg[qb])
                    pkg[b] = O("ve", (lambda e, b=b, qb=qb, pr=pr: e.tensor_copy(
                        s_qtile[qb][:, pr, :], p_q2[b][:, :])), "VE")
                    W("sc", "VE", pkg[b])
                    emit_norm(s_qtile[qb][:, pr, :], 1, pr, s_sqq2, p_nq2, nstq)
                qfree[c] = q_done
                W("gp", "VE", pkg[(NP - 1) % 2])
                W("gp", "VE", pkg[NP % 2])
                qsg[qb] = DMA("gp", qt_sp[:, :, t0:t0 + CH].rearrange("r p t -> p r t"),
                              s_qtile[qb][:], "SP")

            c2_done = O("ve", (lambda e: e.tensor_mul(s_crow[:, :], s_nmax[:, 0, :], s_nmax[:, 1, :])), "VE")
            W("sc", "VE", c2_done)
            ln_done = O("sc", (lambda e: e.activation(s_crow[:, :], s_crow[:, :], AF.Ln)), "SC")
            W("ve", "SC", ln_done)
            aff_done = O("ve", (lambda e: e.tensor_scalar(s_crow[:, :], s_crow[:, :],
                                                          0.5, 0.01, ALU.mult, ALU.add)), "VE")
            W("sc", "VE", aff_done)
            cc_done = O("sc", (lambda e: e.activation(s_crow[:, :], s_crow[:, :], AF.Exp)), "SC")
            W("ve", "SC", cc_done)
            O("ve", (lambda e: e.tensor_scalar_min(s_crow[:, :], s_crow[:, :], 60.0)), "VE")
            cn_done = O("ve", (lambda e: e.tensor_scalar_mul(s_crow[:, :], s_crow[:, :], -1.0)), "VE")
            W("tn", "VE", cn_done)
            ncb = O("tn", (lambda e: e.matmul(p_nc[:, :], s_onesr[:, :], s_crow[:, :],
                                              start=True, stop=True, skip_group_check=True)), "TN")
            W("ve", "TN", ncb)
            O("ve", (lambda e: e.tensor_copy(s_negc[:], p_nc[:, :])), "VE")
            S.barrier()

        with ExitStack() as ph:
            s_mhb = ph.enter_context(nc.sbuf_tensor("s_mhb", [128, NF, D], BF16))
            s_mlb = ph.enter_context(nc.sbuf_tensor("s_mlb", [128, NF, D], BF16))
            s_bias = ph.enter_context(nc.sbuf_tensor("s_bias", [128, NF], F32))
            s_xhb = [ph.enter_context(nc.sbuf_tensor(f"s_xhb{i}", [128, NF, CH], BF16)) for i in range(2)]
            s_xlb = [ph.enter_context(nc.sbuf_tensor(f"s_xlb{i}", [128, NF, CH], BF16)) for i in range(2)]
            s_part = [ph.enter_context(nc.sbuf_tensor(f"s_part{i}", [128, CH], F32)) for i in range(2)]
            p_f = [ph.enter_context(nc.psum_tensor(f"p_f{i}", [128, CH], F32)) for i in range(2)]

            DMA("sync", s_mhb[:], mh_sp.rearrange("i p n -> p i n"), "IN")
            m_loaded = DMA("sync", s_mlb[:], ml_sp.rearrange("i p n -> p i n"), "IN")
            DMA("sync", s_bias[:], biasv[:, :], "IN")
            DMA("sync", s_x[0][:], x_fm.rearrange("i p t -> p i t")[:, :, 0:CH], "IN")
            DMA("sync", s_xhb[0][:], xh_sp.rearrange("i p t -> p i t")[:, :, 0:CH], "IN")
            x_snap[0] = DMA("sync", s_xlb[0][:], xl_sp.rearrange("i p t -> p i t")[:, :, 0:CH], "IN")
            pf_guard = [0, 0]
            spg = [0, 0]
            bfree = [0] * NCH

            for c in range(NCH):
                buf = c % 2
                t0 = c * CH
                if c + 1 < NCH:
                    if c >= 1:
                        W("sync", "TN", bfree[c - 1])
                        W("sync", "VE", xfree_snap[c - 1])
                    DMA("sync", s_x[1 - buf][:],
                        x_fm.rearrange("i p t -> p i t")[:, :, t0 + CH:t0 + 2 * CH], "IN")
                    DMA("sync", s_xhb[1 - buf][:],
                        xh_sp.rearrange("i p t -> p i t")[:, :, t0 + CH:t0 + 2 * CH], "IN")
                    x_snap[c + 1] = DMA("sync", s_xlb[1 - buf][:],
                                        xl_sp.rearrange("i p t -> p i t")[:, :, t0 + CH:t0 + 2 * CH], "IN")
                W("tn", "IN", x_snap[c])
                if c == 0:
                    W("tn", "IN", m_loaded)
                last_pv = 0
                for fo in range(NF):
                    b = fo % 2
                    W("tn", "SC", pf_guard[b])
                    for ti, (xt, mt) in enumerate(((s_xhb[buf], s_mhb), (s_xlb[buf], s_mhb),
                                                   (s_xhb[buf], s_mlb))):
                        for fi in range(NF):
                            f_done = O("tn", (lambda e, fi=fi, fo=fo, b=b, xt=xt, mt=mt,
                                              st=(ti == 0 and fi == 0),
                                              sp=(ti == 2 and fi == NF - 1): e.matmul(
                                p_f[b][:, :], mt[:, fi, fo * 128:(fo + 1) * 128],
                                xt[:, fi, :], start=st, stop=sp)), "TN")
                    W("sc", "TN", f_done)
                    W("sc", "SP", spg[b])
                    pf_guard[b] = O("sc", (lambda e, fo=fo, b=b: e.add(
                        s_part[b][:, :], p_f[b][:, :], s_bias[:, fo:fo + 1])), "SC")
                    W("ve", "SC", pf_guard[b])
                    last_pv = O("ve", (lambda e, fo=fo, b=b, buf=buf: e.tensor_add(
                        s_part[b][:, :], s_part[b][:, :], s_x[buf][:, fo, :])), "VE")
                    W("gp", "VE", last_pv)
                    spg[b] = DMA("gp", part_sp[fo, :, t0:t0 + CH], s_part[b][:], "SP")
                bfree[c] = f_done
                xfree_snap[c] = last_pv
            S.barrier()

    with ExitStack() as ph:
        s_wo = ph.enter_context(nc.sbuf_tensor("s_wo", [128, NF, D], BF16))
        s_qt = [ph.enter_context(nc.sbuf_tensor(f"s_qt{i}", [128, NP, CH], BF16)) for i in range(2)]
        s_pt = [ph.enter_context(nc.sbuf_tensor(f"s_pt{i}", [128, NF, CH], F32)) for i in range(2)]
        s_ring = ph.enter_context(nc.sbuf_tensor("s_ring", [128, 3, 2, KPC, CH], BF16))
        s_mask = ph.enter_context(nc.sbuf_tensor("s_mask", [128, KPC, CH], BF16))
        s_att = ph.enter_context(nc.sbuf_tensor("s_att", [65, H, CH], BF16))
        s_selc = ph.enter_context(nc.sbuf_tensor("s_selc", [65, H, H], BF16))
        s_selb = ph.enter_context(nc.sbuf_tensor("s_selb", [H, NP, 128], F32))
        s_den = ph.enter_context(nc.sbuf_tensor("s_den", [H, CH], F32))
        s_rec = ph.enter_context(nc.sbuf_tensor("s_rec", [H, CH], F32))
        s_afm = ph.enter_context(nc.sbuf_tensor("s_afm", [128, NF, CH], BF16))
        s_out = [ph.enter_context(nc.sbuf_tensor(f"s_out{i}", [128, CH], F32)) for i in range(2)]

        p_sq = [ph.enter_context(nc.psum_tensor(f"p_sq{i}", [128, 2, KPC, CH], F32)) for i in range(3)]
        p_att = ph.enter_context(nc.psum_tensor("p_att", [65, 2, CH], F32))
        p_misc = ph.enter_context(nc.psum_tensor("p_misc", [128, 2, CH], F32))
        p_den = p_misc[0:16, 0, :]
        p_bc = p_misc[:, 1, :]

        wo_snap = DMA("gp", s_wo[:], w_o.rearrange("(i p) n -> p i n", p=128), "SP")
        DMA("sync", s_mask[:], masks.rearrange("j p t -> p j t"), "IN")
        DMA("sync", s_selc[:], selcol[:, :, :], "IN")
        DMA("sync", s_selb[:], selbc[:, :, :], "IN")
        tile_snap = [0] * NCH
        DMA("sync", s_qt[0][:], qt_sp.rearrange("r p t -> p r t")[:, :, 0:CH], "IN")
        tile_snap[0] = DMA("sync", s_pt[0][:], part_sp.rearrange("i p t -> p i t")[:, :, 0:CH], "IN")

        tilefree = [0] * NCH
        expdone = [0, 0, 0]
        ringfree = [0, 0, 0]
        att_evac = 0
        outfree = [0, 0]
        po_guard = [0, 0]
        den_copy = 0
        bc_last = 0

        for c in range(NCH):
            buf = c % 2
            t0 = c * CH
            nkt = KPC * (c + 1)
            if c + 1 < NCH:
                if c >= 1:
                    W("sync", "VE", tilefree[c - 1])
                DMA("sync", s_qt[1 - buf][:],
                    qt_sp.rearrange("r p t -> p r t")[:, :, t0 + CH:t0 + 2 * CH], "IN")
                tile_snap[c + 1] = DMA("sync", s_pt[1 - buf][:],
                                       part_sp.rearrange("i p t -> p i t")[:, :, t0 + CH:t0 + 2 * CH], "IN")
            qt = s_qt[buf]
            W("tn", "IN", tile_snap[c])

            def emit_attnv(pgi, ptiles, psnap, pmask, pr, last):
                nonlocal att_evac
                W("tn", "SC", psnap)
                if pmask is not None:
                    W("tn", "VE", pmask)
                if pgi == 0:
                    W("tn", "VE", att_evac)
                gpar = pgi % 3
                for ji, kt in enumerate(ptiles):
                    first = (pgi == 0 and ji == 0)
                    O("tn", (lambda e, kt=kt, ji=ji, pr=pr, first=first, gpar=gpar: e.matmul(
                        p_att[:, 0, :], v_s[:, kt, 2 * pr, :], s_ring[:, gpar, 0, ji, :],
                        start=first, stop=False, skip_group_check=True)), None)
                    done = O("tn", (lambda e, kt=kt, ji=ji, pr=pr, gpar=gpar,
                                    sp=(last and ji == len(ptiles) - 1): e.matmul(
                        p_att[:, 1, :], v_s[:, kt, 2 * pr + 1, :], s_ring[:, gpar, 1, ji, :],
                        start=False, stop=sp, skip_group_check=True)), "TN")
                ringfree[gpar] = done
                return done

            for pr in range(NP):
                groups = [list(range(g, g + KPC)) for g in range(0, nkt, KPC)]
                prev = None
                for gi, tiles in enumerate(groups):
                    gpar = gi % 3
                    W("tn", "SC", expdone[gpar])
                    if gpar == 0:
                        W("tn", "VE", po_guard[0])
                        W("tn", "VE", po_guard[1])
                    for ji, kt in enumerate(tiles):
                        O("tn", (lambda e, kt=kt, ji=ji, pr=pr, gpar=gpar, qt=qt: e.matmul(
                            p_sq[gpar][:, 0, ji, :], k_s[0:64, pr, kt * KT:(kt + 1) * KT],
                            qt[0:64, pr, :], start=(ji == 0), stop=(ji == KPC - 1),
                            skip_group_check=True)), None)
                        sc_snap = O("tn", (lambda e, kt=kt, ji=ji, pr=pr, gpar=gpar, qt=qt: e.matmul(
                            p_sq[gpar][:, 1, ji, :], k_s[64:128, pr, kt * KT:(kt + 1) * KT],
                            qt[64:128, pr, :], start=(ji == 0), stop=(ji == KPC - 1),
                            skip_group_check=True)), "TN")
                    W("sc", "TN", sc_snap)
                    W("sc", "TN", ringfree[gpar])
                    expdone[gpar] = O("sc", (lambda e, gpar=gpar, pr=pr: e.activation(
                        s_ring[:, gpar, :, :, :], p_sq[gpar][:, :, :, :], AF.Exp,
                        bias=s_negc[:, pr:pr + 1])), "SC")
                    mask_done = None
                    for ji, kt in enumerate(tiles):
                        j = kt - KPC * c
                        if j >= 0:
                            W("ve", "SC", expdone[gpar])
                            O("ve", (lambda e, gpar=gpar, ji=ji, j=j: e.tensor_mul(
                                s_ring[:, gpar, 0, ji, :], s_ring[:, gpar, 0, ji, :],
                                s_mask[:, j, :])), "VE")
                            mask_done = O("ve", (lambda e, gpar=gpar, ji=ji, j=j: e.tensor_mul(
                                s_ring[:, gpar, 1, ji, :], s_ring[:, gpar, 1, ji, :],
                                s_mask[:, j, :])), "VE")
                    if prev is not None:
                        emit_attnv(*prev, pr, False)
                    prev = (gi, tiles, expdone[gpar], mask_done)
                attnv_done = emit_attnv(*prev, pr, True)
                W("ve", "TN", attnv_done)
                O("ve", (lambda e, pr=pr: e.tensor_copy(s_att[:, 2 * pr, :], p_att[:, 0, :])), "VE")
                att_evac = O("ve", (lambda e, pr=pr: e.tensor_copy(
                    s_att[:, 2 * pr + 1, :], p_att[:, 1, :])), "VE")

            W("tn", "VE", att_evac)
            W("tn", "VE", bc_last)
            for h in range(H):
                den_done = O("tn", (lambda e, h=h: e.matmul(
                    p_den, s_selc[:, h, :], s_att[:, h, :],
                    start=(h == 0), stop=(h == H - 1), skip_group_check=True)), "TN")
            W("ve", "TN", den_done)
            den_copy = O("ve", (lambda e: e.tensor_copy(s_den[0:16, :], p_den)), "VE")
            rec_done = O("ve", (lambda e: e.reciprocal(s_rec[0:16, :], s_den[0:16, :])), "VE")
            W("tn", "VE", den_copy)
            bc_guard = rec_done
            for pr in range(NP):
                W("tn", "VE", bc_guard)
                bcm = O("tn", (lambda e, pr=pr: e.matmul(
                    p_bc, s_selb[:, pr, :], s_rec[0:16, :],
                    start=True, stop=True, skip_group_check=True)), "TN")
                W("ve", "TN", bcm)
                O("ve", (lambda e, pr=pr: e.tensor_mul(
                    s_afm[0:64, pr, :], s_att[0:64, 2 * pr, :], p_misc[0:64, 1, :])), "VE")
                bc_guard = O("ve", (lambda e, pr=pr: e.tensor_mul(
                    s_afm[64:128, pr, :], s_att[0:64, 2 * pr + 1, :], p_misc[64:128, 1, :])), "VE")
            bc_last = bc_guard
            if c == 0:
                W("tn", "SP", wo_snap)
            for fo in range(NF):
                b = fo % 2
                if fo == 0:
                    W("tn", "VE", bc_guard)
                    W("tn", "SC", expdone[0])
                W("tn", "VE", po_guard[b])
                for fi in range(NF):
                    wo_done = O("tn", (lambda e, fi=fi, fo=fo, b=b: e.matmul(
                        p_sq[0][:, b, 0, :], s_wo[:, fi, fo * 128:(fo + 1) * 128],
                        s_afm[:, fi, :], start=(fi == 0), stop=(fi == NF - 1),
                        skip_group_check=True)), "TN")
                W("ve", "TN", wo_done)
                W("ve", "IN", tile_snap[c])
                W("ve", "SP", outfree[b])
                og = O("ve", (lambda e, fo=fo, b=b, buf=buf: e.scalar_tensor_tensor(
                    s_out[b][:, :], p_sq[0][:, b, 0, :], 1.0, s_pt[buf][:, fo, :],
                    ALU.mult, ALU.add)), "VE")
                po_guard[b] = og
                W("gp", "VE", og)
                outfree[b] = DMA("gp", out_fm[fo, :, t0:t0 + CH], s_out[b][:], "SP")
            tilefree[c] = po_guard[(NF - 1) % 2]

        S.wait("gp", "SP")

    def replay(eng):
        def run(e):
            for item in S.ops[eng]:
                if item[0] == "w":
                    e.wait_ge(sems[item[1]], item[2])
                else:
                    instr = item[1](e)
                    if item[2] is not None:
                        instr.then_inc(sems[item[2]], item[3])
        return run

    with nc.Block() as block:
        block.sync(replay("sync"))
        block.gpsimd(replay("gp"))
        block.vector(replay("ve"))
        block.scalar(replay("sc"))
        block.tensor(replay("tn"))

    top.close()
    return nc


_NC_CACHE = None


def _get_program():
    global _NC_CACHE
    if _NC_CACHE is None:
        _NC_CACHE = build_program()
    return _NC_CACHE


def kernel(L0_attn_x, L0_attn_W_qkv, L0_attn_W_q_lsr, L0_attn_W_k_lsr, L0_attn_core,
           L0_attn_W_o, L0_ffn_w_freq_real, L0_ffn_w_freq_imag, L0_ffn_bias):
    x = np.asarray(L0_attn_x, np.float32)
    W_qkv = np.ascontiguousarray(np.asarray(L0_attn_W_qkv, np.float32))
    W_qlsr = np.asarray(L0_attn_W_q_lsr, np.float32)
    W_klsr = np.asarray(L0_attn_W_k_lsr, np.float32)
    core = np.asarray(L0_attn_core, np.float32)
    W_o_np = np.ascontiguousarray(np.asarray(L0_attn_W_o, np.float32))
    Wr = np.ascontiguousarray(np.asarray(L0_ffn_w_freq_real, np.float32))
    Wi = np.ascontiguousarray(np.asarray(L0_ffn_w_freq_imag, np.float32))
    biasn = np.asarray(L0_ffn_bias, np.float32)

    jk = np.outer(np.arange(D, dtype=np.float64), np.arange(D, dtype=np.float64))
    ang = 2.0 * np.pi * (jk % D) / D
    C_np = np.ascontiguousarray(np.cos(ang).astype(np.float32))
    S_np = np.ascontiguousarray(np.sin(ang).astype(np.float32))
    masks_np = np.zeros((KPC, 128, CH), BFNP)
    for j in range(KPC):
        for kk in range(128):
            masks_np[j, kk, :] = (np.arange(CH) >= (128 * j + kk)).astype(BFNP)
    selcol_np = np.zeros((65, H, H), BFNP)
    for h in range(H):
        selcol_np[64, h, h] = 1.0
    selbc_np = np.zeros((H, NP, 128), np.float32)
    for pr in range(NP):
        selbc_np[2 * pr, pr, 0:64] = 1.0
        selbc_np[2 * pr + 1, pr, 64:128] = 1.0

    shared = {
        "w_kv": np.ascontiguousarray(W_qkv[:, D:3 * D]),
        "wqT": np.ascontiguousarray(
            W_qkv[:, 0:D].T.reshape(NP, 2, DH, D).transpose(1, 2, 0, 3).reshape(128, NP, D)),
        "qlsrT": np.ascontiguousarray(W_qlsr.transpose(2, 0, 1)),
        "klsrT": np.ascontiguousarray(W_klsr.transpose(2, 0, 1)),
        "coreT": np.ascontiguousarray(core.T),
        "w_o": W_o_np, "w_fr": Wr, "w_fi": Wi,
        "biasv": np.ascontiguousarray(biasn.reshape(NF, 128).T),
        "cmat": C_np, "smat": S_np, "masks": masks_np, "selcol": selcol_np,
        "selbc": selbc_np,
        "onescol": np.full((128, 1), 1.0 / D, np.float32),
        "ones128": np.ones((128, 1), np.float32),
        "onesrow": np.ones((1, 128), np.float32),
        "cneg": np.ascontiguousarray(-C_np),
        "onesbf": np.full((128, 1), 1.0 / D, BFNP),
        "ones1b": np.ones((128, 1), BFNP),
    }
    in_maps = []
    for c in range(NC):
        m = dict(shared)
        m["x_fm"] = np.ascontiguousarray(x[c].T).reshape(NF, 128, T)
        in_maps.append(m)

    res = run_bass_kernel_spmd(_get_program(), in_maps, core_ids=list(range(NC)))
    outs = []
    for c in range(NC):
        o = np.asarray(res.results[c]["out_fm"]).reshape(D, T)
        outs.append(np.ascontiguousarray(o.T))
    return np.stack(outs).astype(np.float32)
